# version 03 (final)
# baseline (speedup 1.0000x reference)
import numpy as np
import concourse.bass as bass
import concourse.mybir as mybir
from concourse.bass_utils import run_bass_kernel_spmd
from concourse.alu_op_type import AluOpType as Op
from concourse.bass_isa import ReduceOp

F32 = mybir.dt.float32
F32R = mybir.dt.float32r
I32 = mybir.dt.int32
AF = mybir.ActivationFunctionType

NCORES = 8
DEBUG = False
B, T, C = 2, 8192, 1024
NT = 2048
NCH = C // 128

def core_tok(c):
    return (c // 4, (c % 4) * NT)


class Sem:
    def __init__(self, nc, name):
        self.h = nc.alloc_semaphore(name)
        self.n = 0

    def inc(self, instr, k=1):
        instr.then_inc(self.h, k)
        self.n += k
        return instr


class Ctx:
    def __init__(self, name):
        self.nc = bass.Bass("TRN2", target_bir_lowering=False)
        self.inputs = {}
        self.outputs = {}
        self._sems = {}

    def sem(self, name):
        if name not in self._sems:
            self._sems[name] = Sem(self.nc, name)
        return self._sems[name]

    def din(self, name, shape, dtype=F32R):
        ap = self.nc.dram_tensor(name, list(shape), dtype, kind="ExternalInput").ap()
        self.inputs[name] = ap
        return ap

    def dout(self, name, shape, dtype=F32R):
        ap = self.nc.dram_tensor(name, list(shape), dtype, kind="ExternalOutput").ap()
        self.outputs[name] = ap
        return ap

    def sb(self, name, shape, dtype=F32R):
        return self.nc.alloc_sbuf_tensor(name, list(shape), dtype).ap()

    def ps(self, name, shape, dtype=F32):
        return self.nc.alloc_psum_tensor(name, list(shape), dtype).ap()

    def _flush_tiles(self):
        if not hasattr(self, "_fl"):
            self._fl = (self.sb("_flv", [1, 512], F32),
                        self.sb("_fla", [1, 512], F32),
                        self.sb("_flb", [1, 512], F32))
        return self._fl

    def chain(self, eng, key, ops):
        ch = self.sem("ch_" + key)
        last = None
        for i, op in enumerate(ops):
            if i > 0:
                eng.wait_ge(ch.h, ch.n)
            last = op()
            ch.inc(last)
        return last

    def vflush(self, v, sem):
        sem.inc(v.drain())

    def init_flush(self, block):
        pass

    def aflush(self, sc, sem):
        sem.inc(sc.drain())


def dma(eng, sem, out, in_):
    sem.inc(eng.dma_start(out=out, in_=in_), 16)



def emit_ln(ctx, block, X, SCR, ROWS, ONESC, ONESR, PS4, sem_pre, sem_pre_val,
            sem_out, eps=1e-5, ntok=NT):
    nc = ctx.nc
    psA, psB, mB, rB = PS4
    NQ = ntok // 512
    sq = ctx.sem("lnsq")
    su = ctx.sem("lnsu")
    sm = ctx.sem("lnmm")
    sr = ctx.sem("lnrw")
    sb = ctx.sem("lnbc")
    sa = ctx.sem("lnap")

    for q in range(NQ):
        ncol = slice(q * 512, (q + 1) * 512)

        @block.scalar
        def _(sc, q=q, ncol=ncol):
            for cch in range(NCH):
                if q == 0 and cch == 0:
                    sc.wait_ge(sem_pre.h, sem_pre_val)
                sc.wait_ge(su.h, max(0, su.n + cch - 1))
                sq.inc(sc.activation(out=SCR[:, cch % 2, :], in_=X[:, cch, ncol],
                                     func=AF.Square))

        @block.tensor
        def _(t, q=q, ncol=ncol):
            if q == 0:
                t.wait_ge(sem_pre.h, sem_pre_val)
            t.wait_ge(sa.h, sa.n)
            for cch in range(NCH):
                last = t.matmul(psA[0:1, :], ONESC, X[:, cch, ncol],
                                start=(cch == 0), stop=(cch == NCH - 1))
            sm.inc(last)
            for cch in range(NCH):
                t.wait_ge(sq.h, sq.n - (NCH - 1) + cch)
                su.inc(t.matmul(psB[0:1, :], ONESC, SCR[:, cch % 2, :],
                                start=(cch == 0), stop=(cch == NCH - 1)))

        MEAN = ROWS[0:1, 0, :]
        RSTD = ROWS[0:1, 1, :]

        @block.vector
        def _(v, q=q):
            v.wait_ge(sm.h, sm.n)
            v.wait_ge(su.h, su.n)
            ctx.chain(v, "V", [
                lambda: v.tensor_scalar_mul(MEAN, psA[0:1, :], 1.0 / C),
                lambda: v.tensor_scalar(RSTD, psB[0:1, :], 1.0 / C, eps, Op.mult, Op.add),
                lambda: v.tensor_tensor(ROWS[0:1, 2, :], MEAN, MEAN, Op.mult),
                lambda: v.tensor_tensor(RSTD, RSTD, ROWS[0:1, 2, :], Op.subtract),
            ])
            ctx.vflush(v, sr)

        @block.scalar
        def _(sc, q=q):
            sc.wait_ge(sr.h, sr.n)
            sc.activation(out=ROWS[0:1, 3, :], in_=RSTD, func=AF.Sqrt)
            ctx.aflush(sc, sr)

        @block.vector
        def _(v, q=q):
            v.wait_ge(sr.h, sr.n)
            with ctx.nc.allow_low_precision(reason="f32r rstd row"):
                v.reciprocal(RSTD, ROWS[0:1, 3, :])
            ctx.vflush(v, sr)

        @block.tensor
        def _(t, q=q):
            t.wait_ge(sr.h, sr.n)
            sb.inc(t.matmul(mB[:, :], ONESR, MEAN, start=True, stop=True))
            sb.inc(t.matmul(rB[:, :], ONESR, RSTD, start=True, stop=True))

        @block.vector
        def _(v, q=q, ncol=ncol):
            v.wait_ge(sb.h, sb.n)
            for cch in range(NCH):
                ctx.chain(v, "V", [
                    lambda cch=cch: v.tensor_tensor(X[:, cch, ncol], X[:, cch, ncol], mB[:, :], Op.subtract),
                    lambda cch=cch: v.tensor_tensor(X[:, cch, ncol], X[:, cch, ncol], rB[:, :], Op.mult),
                ])
            ctx.vflush(v, sa)

    @block.vector
    def _(v):
        v.wait_ge(sa.h, sa.n)
        sem_out.inc(v.tensor_copy(ROWS[0:1, 2, 0:1], ROWS[0:1, 2, 0:1]))



def build_l1ffn():
    ctx = Ctx("l1ffn")
    nc = ctx.nc
    HID = 3072
    MH = HID // 128
    S = 1.0 / 127.0
    BF16 = mybir.dt.bfloat16

    xT = ctx.din("xT", [C, NT])
    c_onesc = ctx.din("c_onesc", [128, 1])
    c_onesr = ctx.din("c_onesr", [1, 128])
    wgT = ctx.din("wgT", [C, HID], I32)
    wuT = ctx.din("wuT", [C, HID], I32)
    wdT = ctx.din("wdT", [HID, C], I32)
    yT = ctx.dout("yT", [C, NT])
    dbgmb = ctx.dout("dbgmb", [128, 8], F32) if DEBUG else None

    X = ctx.sb("X", [128, NCH, NT])
    SQ2 = ctx.sb("SQ2", [128, 2, 512])
    ROWS = ctx.sb("ROWS", [1, 8, 512])
    ONESC = ctx.sb("ONESC", [128, 1])
    ONESR = ctx.sb("ONESR", [1, 128])
    H = ctx.sb("H", [128, MH, NT], BF16)
    WGU = [ctx.sb("wgu0", [128, NCH, 256])]
    WD = [ctx.sb("wd0", [128, MH, 128], BF16)]

    PG = ctx.ps("pg", [128, 512])
    PU = ctx.ps("pu", [128, 512])
    PD = [ctx.ps(f"pd{i}", [128, 512]) for i in range(2)]

    ld = ctx.sem("ld")
    wld = ctx.sem("wld")
    mm = ctx.sem("mm")
    sg = ctx.sem("sg")
    hready = ctx.sem("hready")
    dmm = ctx.sem("dmm")
    ev = ctx.sem("ev")
    fin = ctx.sem("fin")
    st = ctx.sem("st")

    with nc.Block() as block:
        @block.sync
        def _(s):
            dma(s, ld, X[:, :, :], xT.rearrange("(c p) n -> p c n", p=128))
            dma(s, ld, ONESC[:, :], c_onesc[:, :])
            dma(s, ld, ONESR[:, :], c_onesr[:, :])

        for m in range(MH):
            buf = WGU[0]

            @block.gpsimd
            def _(g, m=m, buf=buf):
                g.wait_ge(mm.h, mm.n)
                dma(g, wld, buf[:, :, 0:128],
                    wgT[:, m * 128:(m + 1) * 128].rearrange("(c p) m -> p c m", p=128))
                dma(g, wld, buf[:, :, 128:256],
                    wuT[:, m * 128:(m + 1) * 128].rearrange("(c p) m -> p c m", p=128))

            for q in range(4):
                ncol = slice(q * 512, (q + 1) * 512)

                @block.tensor
                def _(t, m=m, buf=buf, ncol=ncol, q=q):
                    if q == 0:
                        t.wait_ge(ld.h, 48)
                        t.wait_ge(wld.h, wld.n)
                    t.wait_ge(sg.h, sg.n)
                    t.wait_ge(hready.h, hready.n)
                    for k in range(NCH):
                        last = t.matmul(PG[:, :], buf[:, k, 0:128],
                                        X[:, k, ncol], start=(k == 0), stop=(k == NCH - 1))
                    mm.inc(last)
                    for k in range(NCH):
                        last = t.matmul(PU[:, :], buf[:, k, 128:256],
                                        X[:, k, ncol], start=(k == 0), stop=(k == NCH - 1))
                    mm.inc(last)

                @block.scalar
                def _(sc, m=m, ncol=ncol):
                    sc.wait_ge(mm.h, mm.n)
                    sg.inc(sc.activation(out=H[:, m, ncol], in_=PG[:, :],
                                         func=AF.Silu, scale=S))

                @block.vector
                def _(v, m=m, ncol=ncol):
                    v.wait_ge(sg.h, sg.n)
                    hready.inc(v.scalar_tensor_tensor(H[:, m, ncol], PU[:, :], S,
                                                      H[:, m, ncol], Op.mult, Op.mult))

        wld2 = ctx.sem("wld2")
        for oc in range(NCH):
            buf = WD[0]

            @block.gpsimd
            def _(g, oc=oc, buf=buf):
                if oc == 0:
                    g.wait_ge(ctx.sem("hready").h, ctx.sem("hready").n)
                g.wait_ge(dmm.h, dmm.n)
                dma(g, wld2, buf[:, :, :],
                    wdT[:, oc * 128:(oc + 1) * 128].rearrange("(c p) m -> p c m", p=128))

            for q in range(4):
                ncol = slice(q * 512, (q + 1) * 512)
                pb = PD[q % 2]

                @block.tensor
                def _(t, oc=oc, buf=buf, ncol=ncol, pb=pb, q=q):
                    if q == 0:
                        t.wait_ge(hready.h, hready.n)
                        t.wait_ge(wld2.h, wld2.n)
                    t.wait_ge(ev.h, max(0, ev.n - 1))
                    for k in range(MH):
                        last = t.matmul(pb[:, :], buf[:, k, :],
                                        H[:, k, ncol], start=(k == 0), stop=(k == MH - 1))
                    dmm.inc(last)

                @block.vector
                def _(v, oc=oc, ncol=ncol, pb=pb):
                    v.wait_ge(dmm.h, dmm.n)
                    ev.inc(v.scalar_tensor_tensor(X[:, oc, ncol], pb[:, :], S,
                                                  X[:, oc, ncol], Op.mult, Op.add))

    dbg = ctx.dout("dbg", [1, 8, 512]) if DEBUG else None
    with nc.Block() as block:
        emit_ln(ctx, block, X, SQ2, ROWS, ONESC, ONESR, [PG, PU, PD[0], PD[1]], ev, ev.n, fin)

        @block.sync
        def _(s):
            s.wait_ge(fin.h, fin.n)
            dma(s, st, yT.rearrange("(c p) n -> p c n", p=128), X[:, :, :])
            if DEBUG:
                dma(s, st, dbg[:, :, :], ROWS[:, :, :])
            s.wait_ge(st.h, st.n)

    return ctx


def launch_l1ffn(x, wg, wu, wd):
    ctx = build_l1ffn()
    wgT = np.ascontiguousarray(wg.T)
    wuT = np.ascontiguousarray(wu.T)
    wdT = np.ascontiguousarray(wd.T)
    in_maps = []
    for c in range(NCORES):
        b, t0 = core_tok(c)
        in_maps.append({
            "xT": np.ascontiguousarray(x[b, t0:t0 + NT, :].T),
            "wgT": wgT, "wuT": wuT, "wdT": wdT,
            "c_onesc": np.ones((128, 1), np.float32),
            "c_onesr": np.ones((1, 128), np.float32),
        })
    res = run_bass_kernel_spmd(ctx.nc, in_maps, core_ids=list(range(NCORES)))
    out = np.empty_like(x)
    for c in range(NCORES):
        b, t0 = core_tok(c)
        out[b, t0:t0 + NT, :] = res.results[c]["yT"].T
    return out


BF16 = mybir.dt.bfloat16


def emit_projF(ctx, block, pfx, X, WT, out_sb, n_in, n_out, ntok=NT,
               bias_cols=None, wait=None, sem_out=None, add_into=False,
               wbuf=None, out_dtype=None, psums=None):
    msizes = [128] * (n_out // 128) + ([n_out % 128] if n_out % 128 else [])
    """Feature-major projection: out^T[n_out, ntok] = W.T @ x^T (+bias).

    X: sbuf [128, n_in//128, ntok] f32r. WT: DRAM [n_in, n_out] f32r ([K, M]).
    out_sb: sbuf [128, n_out//128, ntok]. bias_cols: sbuf [128, n_out//128] or None.
    wait: (sem, val) gate for first use of X. wbuf: sbuf [128, n_in//128, 128]
    single-buffered weight tile (allocated by caller). add_into: out += result.
    """
    KC = n_in // 128
    MC = len(msizes)
    NQ = ntok // 512
    wld = ctx.sem(pfx + "wld")
    mmd = ctx.sem(pfx + "mm")
    evd = ctx.sem(pfx + "ev")
    if wbuf is None:
        wbuf = ctx.sb(pfx + "wbuf", [128, KC, 128])
    PSP = psums if psums is not None else [ctx.ps(pfx + f"ps{i}", [128, 512]) for i in range(2)]

    cast = (wbuf.dtype != WT.dtype)
    for oc in range(MC):
        mr = msizes[oc]
        if cast:
            @block.gpsimd
            def _(g, oc=oc, mr=mr):
                g.wait_ge(mmd.h, mmd.n)
                dma(g, wld, wbuf[:, :, 0:mr],
                    WT[:, oc * 128:oc * 128 + mr].rearrange("(c p) m -> p c m", p=128))
        else:
            @block.sync
            def _(s, oc=oc, mr=mr):
                s.wait_ge(mmd.h, mmd.n)
                dma(s, wld, wbuf[:, :, 0:mr],
                    WT[:, oc * 128:oc * 128 + mr].rearrange("(c p) m -> p c m", p=128))

        for nq in range(NQ):
            ncol = slice(nq * 512, (nq + 1) * 512)
            pb = PSP[nq % 2]

            @block.tensor
            def _(t, oc=oc, nq=nq, ncol=ncol, pb=pb, mr=mr):
                if nq == 0:
                    t.wait_ge(wld.h, wld.n)
                    if wait is not None and oc == 0:
                        ws = wait if isinstance(wait, list) else [wait]
                        for (wsem, wval) in ws:
                            t.wait_ge(wsem.h, wval)
                t.wait_ge(evd.h, max(0, evd.n - 1))
                for k in range(KC):
                    last = t.matmul(pb[0:mr, :], wbuf[:, k, 0:mr], X[:, k, ncol],
                                    start=(k == 0), stop=(k == KC - 1))
                mmd.inc(last)

            @block.vector
            def _(v, oc=oc, ncol=ncol, pb=pb, mr=mr):
                v.wait_ge(mmd.h, mmd.n)
                dst = out_sb[0:mr, oc, ncol]
                if add_into:
                    if bias_cols is not None:
                        last = v.scalar_tensor_tensor(dst, pb[0:mr, :], bias_cols[0:mr, oc:oc + 1],
                                                      dst, Op.add, Op.add)
                    else:
                        last = v.tensor_tensor(dst, dst, pb[0:mr, :], Op.add)
                elif bias_cols is not None:
                    last = v.tensor_scalar(dst, pb[0:mr, :], bias_cols[0:mr, oc:oc + 1], None, Op.add)
                else:
                    last = v.tensor_copy(dst, pb[0:mr, :])
                evd.inc(last)

    if sem_out is not None:
        @block.vector
        def _(v):
            v.wait_ge(evd.h, evd.n)
            sem_out.inc(v.engine_nop(0))
    return evd



def build_l1attn_a():
    ctx = Ctx("l1attn_a")
    nc = ctx.nc
    xT = ctx.din("xT", [C, NT])
    wqT = ctx.din("wqT", [C, C])
    wkT = ctx.din("wkT", [C, C])
    wvT = ctx.din("wvT", [C, C])
    bq_c = ctx.din("bq_c", [128, NCH], F32)
    bk_c = ctx.din("bk_c", [128, NCH], F32)
    bv_r = ctx.din("bv_r", [1, C])
    c_onesr = ctx.din("c_onesr", [1, 128])
    c_hsel = ctx.din("c_hsel", [128, 2])
    qT = ctx.dout("qT", [C, NT])
    kT = ctx.dout("kT", [C, NT])
    vO = ctx.dout("vO", [NT, C])
    qn = ctx.dout("qn", [1, 16], F32)
    kn = ctx.dout("kn", [1, 16], F32)

    X = ctx.sb("X", [128, NCH, NT])
    QO = ctx.sb("QO", [128, NCH, NT])
    BQ = ctx.sb("BQ", [128, NCH], F32)
    BK = ctx.sb("BK", [128, NCH], F32)
    BVR = ctx.sb("BVR", [1, C])
    ONESR = ctx.sb("ONESR", [1, 128])
    BVB = ctx.sb("BVB", [128, C])
    WV = ctx.sb("WV", [128, NCH, C])
    VS = ctx.sb("VS", [128, 2, C])
    HSEL = ctx.sb("HSEL", [128, 2])
    SQB = ctx.sb("SQB", [128, 512])
    NALL = ctx.sb("NALL", [2, NCH * 4], F32)
    NMAXQ = ctx.sb("NMAXQ", [2, NCH], F32)
    NMAXK = ctx.sb("NMAXK", [2, NCH], F32)
    PBC = [ctx.ps(f"pbc{i}", [128, 512]) for i in range(2)]

    ld = ctx.sem("ld")
    st = ctx.sem("st")

    with nc.Block() as block:
        @block.sync
        def _(s):
            dma(s, ld, X[:, :, :], xT.rearrange("(c p) n -> p c n", p=128))
            dma(s, ld, BQ[:, :], bq_c[:, :])
            dma(s, ld, BK[:, :], bk_c[:, :])
            dma(s, ld, BVR[:, :], bv_r[:, :])
            dma(s, ld, ONESR[:, :], c_onesr[:, :])
            dma(s, ld, WV[:, :, :], wvT.rearrange("(c p) m -> p c m", p=128))
            dma(s, ld, HSEL[:, :], c_hsel[:, :])

        emit_projF(ctx, block, "q", X, wqT, QO, C, C, bias_cols=BQ,
                   wait=(ld, 112))
        @block.sync
        def _(s):
            s.wait_ge(ctx.sem("qev").h, ctx.sem("qev").n)
            dma(s, st, qT.rearrange("(c p) n -> p c n", p=128), QO[:, :, :])
        emit_norms(ctx, block, "qn", QO, HSEL, SQB, NALL, NMAXQ,
                   (ctx.sem("qev"), ctx.sem("qev").n), PBC[0])
        @block.sync
        def _(s):
            s.wait_ge(ctx.sem("qnrd").h, ctx.sem("qnrd").n)
            st.inc(s.dma_start(out=qn.rearrange("a (c p) -> a p c", p=2), in_=NMAXQ[:, :],
                               allow_slow_non_contiguous=True), 16)
        emit_projF(ctx, block, "k", X, wkT, QO, C, C, bias_cols=BK,
                   wait=[(st, st.n), (ctx.sem("qnrd"), ctx.sem("qnrd").n)])
        @block.sync
        def _(s):
            s.wait_ge(ctx.sem("kev").h, ctx.sem("kev").n)
            dma(s, st, kT.rearrange("(c p) n -> p c n", p=128), QO[:, :, :])
        emit_norms(ctx, block, "kn", QO, HSEL, SQB, NALL, NMAXK,
                   (ctx.sem("kev"), ctx.sem("kev").n), PBC[0])
        @block.sync
        def _(s):
            s.wait_ge(ctx.sem("knrd").h, ctx.sem("knrd").n)
            st.inc(s.dma_start(out=kn.rearrange("a (c p) -> a p c", p=2), in_=NMAXK[:, :],
                               allow_slow_non_contiguous=True), 16)

        @block.tensor
        def _(t):
            t.wait_ge(ld.h, 112)
            t.wait_ge(ctx.sem("knrd").h, ctx.sem("knrd").n)
            ctx.sem("vb").inc(t.matmul(PBC[0][:, :], ONESR[:, :], BVR[0:1, 0:512],
                                       start=True, stop=True))
            ctx.sem("vb").inc(t.matmul(PBC[1][:, :], ONESR[:, :], BVR[0:1, 512:1024],
                                       start=True, stop=True))

        @block.vector
        def _(v):
            v.wait_ge(ctx.sem("vb").h, 2)
            ctx.sem("vbc").inc(v.tensor_copy(BVB[:, 0:512], PBC[0][:, :]))
            ctx.sem("vbc").inc(v.tensor_copy(BVB[:, 512:1024], PBC[1][:, :]))

        vmm = ctx.sem("vmm")
        vev = ctx.sem("vev")
        stv = [ctx.sem(f"stv{i}") for i in range(4)]
        for tt in range(NT // 128):
            for half in range(2):
                vcol = slice(half * 512, (half + 1) * 512)
                pb = PBC[half]
                sv = stv[(2 * tt + half) % 4]

                @block.tensor
                def _(t, tt=tt, half=half, vcol=vcol, pb=pb):
                    if tt == 0 and half == 0:
                        t.wait_ge(ctx.sem("vbc").h, 2)
                        t.wait_ge(ctx.sem("kev").h, ctx.sem("kev").n)
                    t.wait_ge(vev.h, max(0, vev.n - 1))
                    for k in range(NCH):
                        last = t.matmul(pb[:, :], X[:, k, tt * 128:(tt + 1) * 128],
                                        WV[:, k, vcol], start=(k == 0), stop=(k == NCH - 1))
                    vmm.inc(last)

                @block.vector
                def _(v, tt=tt, vcol=vcol, pb=pb, sv=sv):
                    v.wait_ge(vmm.h, vmm.n)
                    v.wait_ge(sv.h, sv.n)
                    vev.inc(v.tensor_tensor(VS[:, tt % 2, vcol], pb[:, :], BVB[:, vcol], Op.add))

                @block.sync
                def _(s, tt=tt, vcol=vcol, sv=sv):
                    s.wait_ge(vev.h, vev.n)
                    dma(s, sv, vO[tt * 128:(tt + 1) * 128, vcol], VS[:, tt % 2, vcol])

        @block.sync
        def _(s):
            s.wait_ge(st.h, st.n)
            for sv in stv:
                s.wait_ge(sv.h, sv.n)

    return ctx


def launch_l1attn_a(x, Wq, bq, Wk, bk, Wv, bv, ilv=True):
    ctx = build_l1attn_a()
    wqT = np.ascontiguousarray(Wq.transpose(1, 0, 2).reshape(C, C))
    wkT = np.ascontiguousarray(Wk.transpose(1, 0, 2).reshape(C, C))
    wvT = np.ascontiguousarray(Wv.transpose(1, 0, 2).reshape(C, C))
    bqc = np.ascontiguousarray(bq.reshape(C).reshape(NCH, 128).T)
    bkc = np.ascontiguousarray(bk.reshape(C).reshape(NCH, 128).T)
    bvr = np.ascontiguousarray(bv.reshape(1, C))
    hsel = np.zeros((128, 2), np.float32)
    hsel[0:64, 0] = 1.0
    hsel[64:128, 1] = 1.0
    in_maps = []
    for c in range(NCORES):
        b, sl = core_slices(c, ilv)
        xs = np.concatenate([x[b, s0:s1, :] for (s0, s1) in sl], axis=0)
        in_maps.append({
            "xT": np.ascontiguousarray(xs.T), "wqT": wqT, "wkT": wkT, "wvT": wvT,
            "bq_c": bqc, "bk_c": bkc, "bv_r": bvr,
            "c_onesr": np.ones((1, 128), np.float32),
            "c_hsel": hsel,
        })
    res = run_bass_kernel_spmd(ctx.nc, in_maps, core_ids=list(range(NCORES)))
    qT = np.zeros((B, C, T), np.float32)
    kT = np.zeros((B, C, T), np.float32)
    vv = np.zeros((B, T, C), np.float32)
    qn = np.zeros((NCORES, 16), np.float32)
    kn = np.zeros((NCORES, 16), np.float32)
    for c in range(NCORES):
        b, sl = core_slices(c, ilv)
        o = 0
        for (s0, s1) in sl:
            n = s1 - s0
            qT[b][:, s0:s1] = res.results[c]["qT"][:, o:o + n]
            kT[b][:, s0:s1] = res.results[c]["kT"][:, o:o + n]
            vv[b][s0:s1, :] = res.results[c]["vO"][o:o + n, :]
            o += n
        qn[c] = res.results[c]["qn"][0]
        kn[c] = res.results[c]["kn"][0]
    return qT, kT, vv, qn, kn


def core_slices(c, ilv):
    b = c // 4
    j = c % 4
    if not ilv:
        return b, [(j * NT, (j + 1) * NT)]
    return b, [(j * 1024, (j + 1) * 1024), ((7 - j) * 1024, (8 - j) * 1024)]


def emit_norms(ctx, block, pfx, SRC, HSEL, SQB, NALL, NMAX, wait, psum):
    sqd = ctx.sem(pfx + "sq")
    mmd = ctx.sem(pfx + "mm")
    red = ctx.sem(pfx + "rd")

    for cch in range(NCH):
        for nq in range(NT // 512):
            @block.scalar
            def _(sc, cch=cch, nq=nq):
                if cch == 0 and nq == 0:
                    sc.wait_ge(wait[0].h, wait[1])
                sc.wait_ge(mmd.h, mmd.n)
                sqd.inc(sc.activation(out=SQB[:, :], in_=SRC[:, cch, nq * 512:(nq + 1) * 512],
                                      func=AF.Square))

            @block.tensor
            def _(t, cch=cch, nq=nq):
                t.wait_ge(sqd.h, sqd.n)
                t.wait_ge(red.h, red.n)
                mmd.inc(t.matmul(psum[0:2, :], HSEL[:, :], SQB[:, :], start=True, stop=True))

            @block.vector
            def _(v, cch=cch, nq=nq):
                v.wait_ge(mmd.h, mmd.n)
                i = cch * 4 + nq
                red.inc(v.tensor_reduce(NALL[0:2, i:i + 1], psum[0:2, :],
                                        mybir.AxisListType.X, Op.max))

    @block.vector
    def _(v):
        v.wait_ge(red.h, red.n)
        red.inc(v.tensor_reduce(NMAX[0:2, :], NALL[0:2, :].rearrange("p (c q) -> p c q", q=4),
                                mybir.AxisListType.X, Op.max))


def build_l1attn_b(core, alibi=False, pairs=None, qbsel=None, early=False, noln=False, paranoid=False):
    ctx = Ctx("l1attn_b")
    nc = ctx.nc
    b, slices = core_slices(core, True)
    SC = 0.125

    xT = ctx.din("xT", [C, NT])
    qT = ctx.din("qT", [C, NT])
    kTf = ctx.din("kTf", [C, T])
    vf = ctx.din("vf", [T, 16 * 65])
    woT = ctx.din("woT", [C, C])
    bo_c = ctx.din("bo_c", [128, NCH], F32)
    qn_r = ctx.din("qn_r", [1, 16], F32)
    kn_r = ctx.din("kn_r", [1, 64], F32)
    c_onesc = ctx.din("c_onesc", [128, 1])
    c_onesr = ctx.din("c_onesr", [1, 128])

    yT = ctx.dout("yT", [C, NT])
    dbgmb = ctx.dout("dbgmb", [128, 8], F32) if DEBUG else None

    X = ctx.sb("X", [128, NCH, NT])
    QT = ctx.sb("QT", [128, NCH, NT])
    YT = ctx.sb("YT", [128, NCH, NT], BF16)
    KB = ctx.sb("KB", [128, 2, 128])
    VB = ctx.sb("VB", [128, 2, 130])
    EB = ctx.sb("EB", [128, 2, 1024])
    ZB = EB[:, 1, 0:512]
    ZI = ctx.sb("ZI", [1, 512])
    ONESC = ctx.sb("ONESC", [128, 1])
    ONESR = ctx.sb("ONESR", [1, 128])
    SQ2 = EB[:, :, 0:512]
    ROWS = ctx.sb("ROWS", [1, 4, 512])
    NROW = ctx.sb("NROW", [1, 64], F32)
    FR = ctx.sb("FR", [1, 4, 16], F32)
    SQN = ctx.sb("SQN", [1, 16], F32)
    ROW8 = ctx.sb("ROW8", [1, 8])
    MB = ctx.sb("MB", [128, 8], F32)
    BO = ctx.sb("BO", [128, NCH], F32)

    SP = [ctx.ps(f"sp{i}", [128, 1024]) for i in range(2)]
    YP = [[ctx.ps(f"yp{qc}{hh}", [128, 512]) for hh in range(2)] for qc in range(2)]

    ld = ctx.sem("ld")
    kvp = [ctx.sem(f"kvp{i}") for i in range(4)]
    qkd = ctx.sem("qkd")
    exd = ctx.sem("exd")
    afd = ctx.sem("afd")
    avd = ctx.sem("avd")
    evd = ctx.sem("evd")
    fin = ctx.sem("fin")
    st_ = ctx.sem("st")

    with nc.Block() as block:
        @block.sync
        def _(s):
            dma(s, ld, X[:, :, :], xT.rearrange("(c p) n -> p c n", p=128))
            dma(s, ld, QT[:, :, :], qT.rearrange("(c p) n -> p c n", p=128))
            dma(s, ld, ONESC[:, :], c_onesc[:, :])
            dma(s, ld, ONESR[:, :], c_onesr[:, :])
            dma(s, ld, FR[0:1, 0, :], qn_r[:, :])
            dma(s, ld, NROW[:, :], kn_r[:, :])
            dma(s, ld, BO[:, :], bo_c[:, :])

        ctx.init_flush(block)

        @block.vector
        def _(v):
            v.wait_ge(ld.h, ld.n)
            ctx.chain(v, "V", [
                lambda: v.tensor_tensor(FR[0:1, 1, :], NROW[0:1, 0:16], NROW[0:1, 16:32], Op.max),
                lambda: v.tensor_tensor(FR[0:1, 1, :], FR[0:1, 1, :], NROW[0:1, 32:48], Op.max),
                lambda: v.tensor_tensor(FR[0:1, 1, :], FR[0:1, 1, :], NROW[0:1, 48:64], Op.max),
                lambda: v.tensor_tensor(FR[0:1, 0, :], FR[0:1, 0, :], FR[0:1, 1, :], Op.mult),
            ])
            ctx.vflush(v, ctx.sem("m0a"))

        @block.scalar
        def _(sc):
            sc.wait_ge(ctx.sem("m0a").h, 1)
            sc.activation(out=SQN[0:1, :], in_=FR[0:1, 0, :], func=AF.Sqrt)
            ctx.aflush(sc, ctx.sem("m0b"))

        @block.vector
        def _(v):
            v.wait_ge(ctx.sem("m0b").h, 1)
            ctx.chain(v, "V", [
                lambda: v.tensor_scalar_mul(SQN[0:1, :], SQN[0:1, :], -SC),
                lambda: v.tensor_reduce(FR[0:1, 3, 0:8], SQN.rearrange("a (p h) -> a p h", h=2),
                                        mybir.AxisListType.X, Op.min),
                lambda: v.tensor_copy(ROW8[0:1, :], FR[0:1, 3, 0:8]),
            ])
            ctx.vflush(v, ctx.sem("m0c"))

        @block.tensor
        def _(t):
            t.wait_ge(ctx.sem("m0c").h, 1)
            ctx.sem("m0d").inc(t.matmul(SP[0][:, 0:8], ONESR[:, :], ROW8[0:1, :],
                                        start=True, stop=True))

        @block.vector
        def _(v):
            v.wait_ge(ctx.sem("m0d").h, 1)
            v.tensor_copy(MB[:, :], SP[0][:, 0:8])
            ctx.vflush(v, ctx.sem("m0e"))
        if DEBUG:
            @block.sync
            def _(s):
                s.wait_ge(ctx.sem("m0e").h, 1)
                dma(s, ctx.sem("dbgst"), dbgmb[:, :], MB[:, :])

        avd_after = {}
        first_qk = [True]
        PAIRS = list(range(8)) if pairs is None else pairs
        for qbi, (q0g, q1g) in enumerate(slices):
            if qbsel is not None and qbi not in qbsel:
                continue
            NST = q1g // 128
            lim = [(q0g + 512) // 128, (q0g + 1024) // 128]
            for pair in PAIRS:
                for sti in range(NST):
                    kv = kvp[sti % 4]

                    @block.sync
                    def _(s, sti=sti, pair=pair, qbi=qbi, kv=kv):
                        key = (qbi, pair, sti - 2)
                        if key in avd_after:
                            s.wait_ge(avd.h, avd_after[key])
                        elif sti - 2 < 0 and (qbi, pair) != (0, 0):
                            s.wait_ge(avd.h, avd.n)
                        dma(s, kv, KB[:, sti % 2, :],
                            kTf[pair * 128:(pair + 1) * 128, sti * 128:(sti + 1) * 128])
                        dma(s, kv, VB[:, sti % 2, :],
                            vf[sti * 128:(sti + 1) * 128, pair * 130:(pair + 1) * 130])

                    for qc in range(2):
                        if sti >= lim[qc]:
                            continue
                        qcol = slice(qbi * 1024 + qc * 512, qbi * 1024 + (qc + 1) * 512)

                        if paranoid:
                            allsems = [kvld, qkd, exd, afd, avd, evd]
                            @block.tensor
                            def _(t):
                                for ss in allsems:
                                    t.wait_ge(ss.h, ss.n)
                            @block.scalar
                            def _(sc):
                                for ss in allsems:
                                    sc.wait_ge(ss.h, ss.n)
                            @block.vector
                            def _(v):
                                for ss in allsems:
                                    v.wait_ge(ss.h, ss.n)
                            @block.gpsimd
                            def _(g2):
                                for ss in allsems:
                                    g2.wait_ge(ss.h, ss.n)

                        @block.tensor
                        def _(t, sti=sti, qc=qc, pair=pair, qcol=qcol, kv=kv):
                            if first_qk[0]:
                                t.wait_ge(ctx.sem("m0e").h, 1)
                                first_qk[0] = False
                            t.wait_ge(kv.h, kv.n)
                            t.wait_ge(exd.h, exd.n)
                            if sti == 0:
                                t.wait_ge(evd.h, evd.n)
                            qkd.inc(t.matmul(SP[qc][:, 0:512], KB[0:64, sti % 2, :],
                                             QT[0:64, pair, qcol], start=True, stop=True,
                                             tile_position=(0, 0)))
                            qkd.inc(t.matmul(SP[qc][:, 512:1024], KB[64:128, sti % 2, :],
                                             QT[64:128, pair, qcol], start=True, stop=True,
                                             tile_position=(64, 0)))

                        @block.scalar
                        def _(sc, sti=sti, qc=qc, pair=pair):
                            sc.wait_ge(qkd.h, qkd.n)
                            sc.wait_ge(avd.h, avd.n)
                            sc.wait_ge(evd.h, evd.n)
                            exd.inc(sc.activation(out=EB[:, qc, :], in_=SP[qc][:, :],
                                                  func=AF.Exp, bias=MB[:, pair:pair + 1],
                                                  scale=SC))

                        diag = (sti >= lim[qc] - 4)
                        if diag:
                            @block.gpsimd
                            def _(g, sti=sti, qc=qc, q0g=q0g):
                                g.wait_ge(exd.h, exd.n)
                                afd.inc(g.affine_select(
                                    EB[:, qc, :], EB[:, qc, :],
                                    [[0, 2], [1, 512]], Op.is_ge, 0.0,
                                    base=q0g + qc * 512 - sti * 128,
                                    channel_multiplier=-1))

                        @block.tensor
                        def _(t, sti=sti, qc=qc, diag=diag):
                            if diag:
                                t.wait_ge(afd.h, afd.n)
                            else:
                                t.wait_ge(exd.h, exd.n)
                            if sti == 0:
                                t.wait_ge(evd.h, evd.n)
                            first = (sti == 0)
                            lastst = (sti == lim[qc] - 1)
                            avd.inc(t.matmul(YP[qc][0][0:65, :], VB[:, sti % 2, 0:65],
                                             EB[:, qc, 0:512], start=first, stop=lastst,
                                             skip_group_check=True))
                            avd.inc(t.matmul(YP[qc][1][0:65, :], VB[:, sti % 2, 65:130],
                                             EB[:, qc, 512:1024], start=first, stop=lastst,
                                             skip_group_check=True))

                    avd_after[(qbi, pair, sti)] = avd.n

                for qc in range(2):
                    qcol = slice(qbi * 1024 + qc * 512, qbi * 1024 + (qc + 1) * 512)
                    for hh in range(2):
                        @block.vector
                        def _(v, qc=qc, hh=hh):
                            v.wait_ge(avd.h, avd.n)
                            with nc.allow_low_precision(reason="1/Z row"):
                                v.reciprocal(ZI[0:1, :], YP[qc][hh][64:65, :])
                            ctx.vflush(v, ctx.sem("zi"))

                        @block.tensor
                        def _(t):
                            t.wait_ge(ctx.sem("zi").h, ctx.sem("zi").n)
                            ctx.sem("zb").inc(t.matmul(SP[0][:, 0:512], ONESR[:, :], ZI[0:1, :],
                                                       start=True, stop=True))

                        @block.vector
                        def _(v, qc=qc, hh=hh, pair=pair, qcol=qcol):
                            v.wait_ge(ctx.sem("zb").h, ctx.sem("zb").n)
                            ctx.chain(v, "V", [
                                lambda: v.tensor_copy(ZB[:, :], SP[0][:, 0:512]),
                                lambda qc=qc, hh=hh, pair=pair, qcol=qcol: v.tensor_tensor(
                                    YT[hh * 64:(hh + 1) * 64, pair, qcol],
                                    YP[qc][hh][0:64, :], ZB[0:64, :], Op.mult),
                            ])
                            ctx.vflush(v, evd)

        if early:
            @block.gpsimd
            def _(g):
                g.wait_ge(evd.h, evd.n)
                dma(g, st_, yT.rearrange("(c p) n -> p c n", p=128), YT[:, :, :])
            @block.sync
            def _(s):
                s.wait_ge(st_.h, st_.n)
            return ctx
        OWB = ctx.sb("owbuf", [128, NCH, 128], BF16)
        emit_projF(ctx, block, "o", YT, woT, X, C, C, bias_cols=BO,
                   wait=(evd, evd.n), add_into=True, wbuf=OWB,
                   psums=[SP[0][:, 0:512], SP[0][:, 512:1024]])

        oev = ctx.sem("oev")
        if noln:
            @block.vector
            def _(v):
                v.wait_ge(oev.h, oev.n)
                fin.inc(v.tensor_copy(ROWS[0:1, 0, 0:1], ONESC[0:1, 0:1]))
        else:
            emit_ln(ctx, block, X, SQ2, ROWS, ONESC, ONESR,
                    [SP[1][:, 0:512], SP[1][:, 512:1024], YP[0][0], YP[0][1]],
                    ctx.sem("oev"), ctx.sem("oev").n, fin)

        @block.sync
        def _(s):
            s.wait_ge(fin.h, fin.n)
            dma(s, st_, yT.rearrange("(c p) n -> p c n", p=128), X[:, :, :])
            s.wait_ge(st_.h, st_.n)

    return ctx


def launch_l1attn_b(x, qT, kT, vv, qn, kn, Wo, bo):
    woT = np.ascontiguousarray(Wo.T)
    boc = np.ascontiguousarray(bo.reshape(NCH, 128).T)
    v65 = np.concatenate([vv.reshape(B, T, 16, 64),
                          np.ones((B, T, 16, 1), np.float32)], axis=-1).reshape(B, T, 16 * 65)
    out = np.empty_like(x)
    for j in range(4):
        ctx = build_l1attn_b(j)
        maps = []
        for b_ in range(B):
            c = b_ * 4 + j
            bb, sl = core_slices(c, True)
            xs = np.concatenate([x[bb, s0:s1, :] for (s0, s1) in sl], axis=0)
            qs = np.concatenate([qT[bb][:, s0:s1] for (s0, s1) in sl], axis=1)
            kn_batch = kn[bb * 4:(bb + 1) * 4].reshape(1, 64)
            maps.append({
                "xT": np.ascontiguousarray(xs.T), "qT": np.ascontiguousarray(qs),
                "kTf": kT[bb], "vf": v65[bb], "woT": woT, "bo_c": boc,
                "qn_r": qn[c:c + 1], "kn_r": kn_batch,
                "c_onesc": np.ones((128, 1), np.float32),
                "c_onesr": np.ones((1, 128), np.float32),
            })
        res = run_bass_kernel_spmd(ctx.nc, maps, core_ids=[j, 4 + j])
        for b_ in range(B):
            c = b_ * 4 + j
            bb, sl = core_slices(c, True)
            o = 0
            for (s0, s1) in sl:
                n = s1 - s0
                out[bb, s0:s1, :] = res.results[b_]["yT"][:, o:o + n].T
                o += n
    return out



def build_l0attn_a():
    ctx = Ctx("l0attn_a")
    nc = ctx.nc
    xT = ctx.din("xT", [C, NT])
    wqdT = ctx.din("wqdT", [C, 512])
    wqkT = ctx.din("wqkT", [512, C])
    wkvT = ctx.din("wkvT", [C, 64])
    wvuT = ctx.din("wvuT", [64, C])
    bqd_c = ctx.din("bqd_c", [128, 4], F32)
    bqk_c = ctx.din("bqk_c", [128, NCH], F32)
    bkv_c = ctx.din("bkv_c", [128, 1], F32)
    bvu_r = ctx.din("bvu_r", [1, C])
    c_onesr = ctx.din("c_onesr", [1, 128])
    c_onesc = ctx.din("c_onesc", [128, 1])
    c_hsel = ctx.din("c_hsel", [128, 2])
    qT = ctx.dout("qT", [C, NT])
    ckvT = ctx.dout("ckvT", [64, NT])
    vO = ctx.dout("vO", [NT, C])
    qn = ctx.dout("qn", [1, 16], F32)
    kn = ctx.dout("kn", [1, 1], F32)

    X = ctx.sb("X", [128, NCH, NT])
    CQ = ctx.sb("CQ", [128, 4, NT])
    QO = ctx.sb("QO", [128, NCH, NT])
    CKV = ctx.sb("CKV", [128, 1, NT])
    WVU = ctx.sb("WVU", [128, 1, C])
    BQD = ctx.sb("BQD", [128, 4], F32)
    BQK = ctx.sb("BQK", [128, NCH], F32)
    BKV = ctx.sb("BKV", [128, 1], F32)
    BVR = ctx.sb("BVR", [1, C])
    BVB = ctx.sb("BVB", [128, C])
    VS = ctx.sb("VS", [128, 2, C])
    ONESR = ctx.sb("ONESR", [1, 128])
    ONESC = ctx.sb("ONESC", [128, 1])
    HSEL = ctx.sb("HSEL", [128, 2])
    SQB = ctx.sb("SQB", [128, 512])
    NALL = ctx.sb("NALL", [2, NCH * 4], F32)
    NMAXQ = ctx.sb("NMAXQ", [2, NCH], F32)
    KALL = ctx.sb("KALL", [1, 4], F32)
    KMAX = ctx.sb("KMAX", [1, 1], F32)
    PBC = [ctx.ps(f"pbc{i}", [128, 512]) for i in range(2)]

    ld = ctx.sem("ld")
    st = ctx.sem("st")

    with nc.Block() as block:
        @block.sync
        def _(s):
            dma(s, ld, X[:, :, :], xT.rearrange("(c p) n -> p c n", p=128))
            dma(s, ld, BQD[:, :], bqd_c[:, :])
            dma(s, ld, BQK[:, :], bqk_c[:, :])
            dma(s, ld, BKV[:, :], bkv_c[:, :])
            dma(s, ld, BVR[:, :], bvu_r[:, :])
            dma(s, ld, ONESR[:, :], c_onesr[:, :])
            dma(s, ld, ONESC[:, :], c_onesc[:, :])
            dma(s, ld, HSEL[:, :], c_hsel[:, :])
            dma(s, ld, WVU[0:64, 0, :], wvuT.rearrange("(c p) m -> p c m", p=64)[:, 0, :])

        emit_projF(ctx, block, "cq", X, wqdT, CQ, C, 512, bias_cols=BQD,
                   wait=(ld, ld.n))
        emit_projF(ctx, block, "q", CQ, wqkT, QO, 512, C, bias_cols=BQK,
                   wait=(ctx.sem("cqev"), ctx.sem("cqev").n))
        @block.sync
        def _(s):
            s.wait_ge(ctx.sem("qev").h, ctx.sem("qev").n)
            dma(s, st, qT.rearrange("(c p) n -> p c n", p=128), QO[:, :, :])
        emit_norms(ctx, block, "qn", QO, HSEL, SQB, NALL, NMAXQ,
                   (ctx.sem("qev"), ctx.sem("qev").n), PBC[0])
        @block.sync
        def _(s):
            s.wait_ge(ctx.sem("qnrd").h, ctx.sem("qnrd").n)
            st.inc(s.dma_start(out=qn.rearrange("a (c p) -> a p c", p=2), in_=NMAXQ[:, :],
                               allow_slow_non_contiguous=True), 16)

        emit_projF(ctx, block, "kv", X, wkvT, CKV, C, 64, bias_cols=BKV,
                   wait=(ctx.sem("qnrd"), ctx.sem("qnrd").n))
        @block.sync
        def _(s):
            s.wait_ge(ctx.sem("kvev").h, ctx.sem("kvev").n)
            dma(s, st, ckvT.rearrange("(c p) n -> p c n", p=64)[:, 0, :], CKV[0:64, 0, :])

        kq = ctx.sem("kq"); km = ctx.sem("km"); kr = ctx.sem("kr")
        for nq in range(NT // 512):
            @block.scalar
            def _(sc, nq=nq):
                if nq == 0:
                    sc.wait_ge(ctx.sem("kvev").h, ctx.sem("kvev").n)
                sc.wait_ge(km.h, km.n)
                kq.inc(sc.activation(out=SQB[0:64, :], in_=CKV[0:64, 0, nq * 512:(nq + 1) * 512],
                                     func=AF.Square))
            @block.tensor
            def _(t, nq=nq):
                t.wait_ge(kq.h, kq.n)
                t.wait_ge(kr.h, kr.n)
                km.inc(t.matmul(PBC[0][0:1, :], ONESC[0:64, :], SQB[0:64, :],
                                start=True, stop=True))
            @block.vector
            def _(v, nq=nq):
                v.wait_ge(km.h, km.n)
                kr.inc(v.tensor_reduce(KALL[0:1, nq:nq + 1], PBC[0][0:1, :],
                                       mybir.AxisListType.X, Op.max))
        @block.vector
        def _(v):
            v.wait_ge(kr.h, kr.n)
            v.tensor_reduce(KMAX[0:1, :], KALL[0:1, :], mybir.AxisListType.X, Op.max)
            ctx.vflush(v, kr)
        @block.sync
        def _(s):
            s.wait_ge(kr.h, kr.n)
            dma(s, st, kn[:, :], KMAX[:, :])

        @block.tensor
        def _(t):
            t.wait_ge(ld.h, ld.n)
            t.wait_ge(kr.h, kr.n)
            ctx.sem("vb").inc(t.matmul(PBC[0][:, :], ONESR[:, :], BVR[0:1, 0:512],
                                       start=True, stop=True))
            ctx.sem("vb").inc(t.matmul(PBC[1][:, :], ONESR[:, :], BVR[0:1, 512:1024],
                                       start=True, stop=True))
        @block.vector
        def _(v):
            v.wait_ge(ctx.sem("vb").h, 2)
            ctx.sem("vbc").inc(v.tensor_copy(BVB[:, 0:512], PBC[0][:, :]))
            ctx.sem("vbc").inc(v.tensor_copy(BVB[:, 512:1024], PBC[1][:, :]))
        vmm = ctx.sem("vmm")
        vev = ctx.sem("vev")
        stv = [ctx.sem(f"stv{i}") for i in range(4)]
        for tt in range(NT // 128):
            for half in range(2):
                vcol = slice(half * 512, (half + 1) * 512)
                pb = PBC[half]
                sv = stv[(2 * tt + half) % 4]

                @block.tensor
                def _(t, tt=tt, half=half, vcol=vcol, pb=pb):
                    if tt == 0 and half == 0:
                        t.wait_ge(ctx.sem("vbc").h, 2)
                    t.wait_ge(vev.h, max(0, vev.n - 1))
                    vmm.inc(t.matmul(pb[:, :], CKV[0:64, 0, tt * 128:(tt + 1) * 128],
                                     WVU[0:64, 0, vcol], start=True, stop=True))

                @block.vector
                def _(v, tt=tt, vcol=vcol, pb=pb, sv=sv):
                    v.wait_ge(vmm.h, vmm.n)
                    v.wait_ge(sv.h, sv.n)
                    vev.inc(v.tensor_tensor(VS[:, tt % 2, vcol], pb[:, :], BVB[:, vcol], Op.add))

                @block.sync
                def _(s, tt=tt, vcol=vcol, sv=sv):
                    s.wait_ge(vev.h, vev.n)
                    dma(s, sv, vO[tt * 128:(tt + 1) * 128, vcol], VS[:, tt % 2, vcol])

        @block.sync
        def _(s):
            s.wait_ge(st.h, st.n)
            for sv in stv:
                s.wait_ge(sv.h, sv.n)

    return ctx


def build_l0attn_b(core):
    ctx = Ctx("l0attn_b")
    nc = ctx.nc
    b, slices = core_slices(core, True)
    SC = 0.125

    xT = ctx.din("xT", [C, NT])
    qT = ctx.din("qT", [C, NT])
    ckvTf = ctx.din("ckvTf", [64, T])
    vf = ctx.din("vf", [T, 16 * 65])
    wpT = ctx.din("wpT", [C, C])
    bp_c = ctx.din("bp_c", [128, NCH], F32)
    qn_r = ctx.din("qn_r", [1, 16], F32)
    kn_r = ctx.din("kn_r", [1, 4], F32)
    c_onesc = ctx.din("c_onesc", [128, 1])
    c_onesr = ctx.din("c_onesr", [1, 128])
    yT = ctx.dout("yT", [C, NT])

    X = ctx.sb("X", [128, NCH, NT])
    QT = ctx.sb("QT", [128, NCH, NT], BF16)
    YT = ctx.sb("YT", [128, NCH, NT], BF16)
    CKV2 = ctx.sb("CKV2", [128, T // 128, 128], BF16)
    VB = ctx.sb("VB", [128, 2, 130])
    EB = ctx.sb("EB", [128, 2, 1024])
    ZB = EB[:, 1, 0:512]
    ZI = ctx.sb("ZI", [1, 512])
    ONESC = ctx.sb("ONESC", [128, 1])
    ONESR = ctx.sb("ONESR", [1, 128])
    SQ2 = EB[:, :, 0:512]
    ROWS = ctx.sb("ROWS", [1, 4, 512])
    FR = ctx.sb("FR", [1, 4, 16], F32)
    SQN = ctx.sb("SQN", [1, 16], F32)
    ROW8 = ctx.sb("ROW8", [1, 8])
    MB = ctx.sb("MB", [128, 8], F32)
    BP = ctx.sb("BP", [128, NCH], F32)

    SP = [ctx.ps(f"sp{i}", [128, 1024]) for i in range(2)]
    YP = [[ctx.ps(f"yp{qc}{hh}", [128, 512]) for hh in range(2)] for qc in range(2)]

    ld = ctx.sem("ld")
    vp = [ctx.sem(f"vp{i}") for i in range(4)]
    qkd = ctx.sem("qkd")
    exd = ctx.sem("exd")
    avd = ctx.sem("avd")
    evd = ctx.sem("evd")
    fin = ctx.sem("fin")
    st_ = ctx.sem("st")

    with nc.Block() as block:
        @block.sync
        def _(s):
            dma(s, ld, X[:, :, :], xT.rearrange("(c p) n -> p c n", p=128))
            dma(s, ld, ONESC[:, :], c_onesc[:, :])
            dma(s, ld, ONESR[:, :], c_onesr[:, :])
            dma(s, ld, FR[0:1, 0, :], qn_r[:, :])
            dma(s, ld, FR[0:1, 1, 0:4], kn_r[:, :])
            dma(s, ld, BP[:, :], bp_c[:, :])

        @block.gpsimd
        def _(g):
            dma(g, ld, QT[:, :, :], qT.rearrange("(c p) n -> p c n", p=128))
            dma(g, ld, CKV2[0:64, :, :], ckvTf.rearrange("p (c n) -> p c n", n=128))
            dma(g, ld, CKV2[64:128, :, :], ckvTf.rearrange("p (c n) -> p c n", n=128))

        @block.vector
        def _(v):
            v.wait_ge(ld.h, ld.n)
            ctx.chain(v, "V", [
                lambda: v.tensor_tensor(FR[0:1, 2, 0:2], FR[0:1, 1, 0:2], FR[0:1, 1, 2:4], Op.max),
                lambda: v.tensor_tensor(FR[0:1, 2, 0:1], FR[0:1, 2, 0:1], FR[0:1, 2, 1:2], Op.max),
                lambda: v.tensor_scalar(FR[0:1, 0, :], FR[0:1, 0, :], FR[0:1, 2, 0:1], None, Op.mult),
            ])
            ctx.vflush(v, ctx.sem("m0a"))

        @block.scalar
        def _(sc):
            sc.wait_ge(ctx.sem("m0a").h, 1)
            sc.activation(out=SQN[0:1, :], in_=FR[0:1, 0, :], func=AF.Sqrt)
            ctx.aflush(sc, ctx.sem("m0b"))

        @block.vector
        def _(v):
            v.wait_ge(ctx.sem("m0b").h, 1)
            ctx.chain(v, "V", [
                lambda: v.tensor_scalar_mul(SQN[0:1, :], SQN[0:1, :], -SC),
                lambda: v.tensor_reduce(FR[0:1, 3, 0:8], SQN.rearrange("a (p h) -> a p h", h=2),
                                        mybir.AxisListType.X, Op.min),
                lambda: v.tensor_copy(ROW8[0:1, :], FR[0:1, 3, 0:8]),
            ])
            ctx.vflush(v, ctx.sem("m0c"))

        @block.tensor
        def _(t):
            t.wait_ge(ctx.sem("m0c").h, 1)
            ctx.sem("m0d").inc(t.matmul(SP[0][:, 0:8], ONESR[:, :], ROW8[0:1, :],
                                        start=True, stop=True))

        @block.vector
        def _(v):
            v.wait_ge(ctx.sem("m0d").h, 1)
            v.tensor_copy(MB[:, :], SP[0][:, 0:8])
            ctx.vflush(v, ctx.sem("m0e"))

        avd_after = {}
        first_qk = [True]
        NST = T // 128
        for qbi in range(2):
            for pair in range(8):
                for sti in range(NST):
                    vv_ = vp[sti % 4]

                    @block.sync
                    def _(s, sti=sti, pair=pair, qbi=qbi, vv_=vv_):
                        key = (qbi, pair, sti - 2)
                        if key in avd_after:
                            s.wait_ge(avd.h, avd_after[key])
                        elif sti - 2 < 0 and (qbi, pair) != (0, 0):
                            s.wait_ge(avd.h, avd.n)
                        dma(s, vv_, VB[:, sti % 2, :],
                            vf[sti * 128:(sti + 1) * 128, pair * 130:(pair + 1) * 130])

                    for qc in range(2):
                        qcol = slice(qbi * 1024 + qc * 512, qbi * 1024 + (qc + 1) * 512)

                        @block.tensor
                        def _(t, sti=sti, qc=qc, pair=pair, qcol=qcol, vv_=vv_):
                            if first_qk[0]:
                                t.wait_ge(ctx.sem("m0e").h, 1)
                                first_qk[0] = False
                            t.wait_ge(vv_.h, vv_.n)
                            t.wait_ge(exd.h, exd.n)
                            if sti == 0:
                                t.wait_ge(evd.h, evd.n)
                            qkd.inc(t.matmul(SP[qc][:, 0:512], CKV2[0:64, sti, :],
                                             QT[0:64, pair, qcol], start=True, stop=True,
                                             tile_position=(0, 0)))
                            qkd.inc(t.matmul(SP[qc][:, 512:1024], CKV2[64:128, sti, :],
                                             QT[64:128, pair, qcol], start=True, stop=True,
                                             tile_position=(64, 0)))

                        @block.scalar
                        def _(sc, sti=sti, qc=qc, pair=pair):
                            sc.wait_ge(qkd.h, qkd.n)
                            sc.wait_ge(avd.h, avd.n)
                            sc.wait_ge(evd.h, evd.n)
                            exd.inc(sc.activation(out=EB[:, qc, :], in_=SP[qc][:, :],
                                                  func=AF.Exp, bias=MB[:, pair:pair + 1],
                                                  scale=SC))

                        @block.tensor
                        def _(t, sti=sti, qc=qc):
                            t.wait_ge(exd.h, exd.n)
                            if sti == 0:
                                t.wait_ge(evd.h, evd.n)
                            first = (sti == 0)
                            lastst = (sti == NST - 1)
                            avd.inc(t.matmul(YP[qc][0][0:65, :], VB[:, sti % 2, 0:65],
                                             EB[:, qc, 0:512], start=first, stop=lastst,
                                             skip_group_check=True))
                            avd.inc(t.matmul(YP[qc][1][0:65, :], VB[:, sti % 2, 65:130],
                                             EB[:, qc, 512:1024], start=first, stop=lastst,
                                             skip_group_check=True))

                    avd_after[(qbi, pair, sti)] = avd.n

                for qc in range(2):
                    qcol = slice(qbi * 1024 + qc * 512, qbi * 1024 + (qc + 1) * 512)
                    for hh in range(2):
                        @block.vector
                        def _(v, qc=qc, hh=hh):
                            v.wait_ge(avd.h, avd.n)
                            with nc.allow_low_precision(reason="1/Z row"):
                                v.reciprocal(ZI[0:1, :], YP[qc][hh][64:65, :])
                            ctx.vflush(v, ctx.sem("zi"))

                        @block.tensor
                        def _(t):
                            t.wait_ge(ctx.sem("zi").h, ctx.sem("zi").n)
                            ctx.sem("zb").inc(t.matmul(SP[0][:, 0:512], ONESR[:, :], ZI[0:1, :],
                                                       start=True, stop=True))

                        @block.vector
                        def _(v, qc=qc, hh=hh, pair=pair, qcol=qcol):
                            v.wait_ge(ctx.sem("zb").h, ctx.sem("zb").n)
                            ctx.chain(v, "V", [
                                lambda: v.tensor_copy(ZB[:, :], SP[0][:, 0:512]),
                                lambda qc=qc, hh=hh, pair=pair, qcol=qcol: v.tensor_tensor(
                                    YT[hh * 64:(hh + 1) * 64, pair, qcol],
                                    YP[qc][hh][0:64, :], ZB[0:64, :], Op.mult),
                            ])
                            ctx.vflush(v, evd)

        OWB = ctx.sb("owbuf", [128, NCH, 128], BF16)
        emit_projF(ctx, block, "o", YT, wpT, X, C, C, bias_cols=BP,
                   wait=(evd, evd.n), add_into=True, wbuf=OWB,
                   psums=[SP[0][:, 0:512], SP[0][:, 512:1024]])
        emit_ln(ctx, block, X, SQ2, ROWS, ONESC, ONESR,
                [SP[1][:, 0:512], SP[1][:, 512:1024], YP[0][0], YP[0][1]],
                ctx.sem("oev"), ctx.sem("oev").n, fin)

        @block.sync
        def _(s):
            s.wait_ge(fin.h, fin.n)
            dma(s, st_, yT.rearrange("(c p) n -> p c n", p=128), X[:, :, :])
            s.wait_ge(st_.h, st_.n)

    return ctx


def launch_l0attn(x, W):
    ctxa = build_l0attn_a()
    wqdT = np.ascontiguousarray(W['wq_d'].T)
    wqkT = np.ascontiguousarray(W['w_qk'].T)
    wkvT = np.ascontiguousarray(W['wkv_d'].T)
    wvuT = np.ascontiguousarray(W['wv_u'].T)
    bqd = np.ascontiguousarray(W['bq_d'].reshape(4, 128).T)
    bqk = np.ascontiguousarray(W['b_qk'].reshape(NCH, 128).T)
    bkv = np.ascontiguousarray(W['bkv_d'].reshape(1, 64))
    bkv_c = np.zeros((128, 1), np.float32); bkv_c[0:64, 0] = W['bkv_d']
    bvu = np.ascontiguousarray(W['bv_u'].reshape(1, C))
    hsel = np.zeros((128, 2), np.float32); hsel[0:64, 0] = 1; hsel[64:128, 1] = 1
    in_maps = []
    for c in range(NCORES):
        bb, sl = core_slices(c, True)
        xs = np.concatenate([x[bb, s0:s1, :] for (s0, s1) in sl], axis=0)
        in_maps.append({"xT": np.ascontiguousarray(xs.T), "wqdT": wqdT, "wqkT": wqkT,
                        "wkvT": wkvT, "wvuT": wvuT, "bqd_c": bqd, "bqk_c": bqk,
                        "bkv_c": bkv_c, "bvu_r": bvu,
                        "c_onesr": np.ones((1, 128), np.float32),
                        "c_onesc": np.ones((128, 1), np.float32), "c_hsel": hsel})
    res = run_bass_kernel_spmd(ctxa.nc, in_maps, core_ids=list(range(NCORES)))
    qTf = np.zeros((B, C, T), np.float32)
    ckvf = np.zeros((B, 64, T), np.float32)
    vvf = np.zeros((B, T, C), np.float32)
    qn = np.zeros((NCORES, 16), np.float32)
    kn = np.zeros((NCORES, 1), np.float32)
    for c in range(NCORES):
        bb, sl = core_slices(c, True)
        o = 0
        for (s0, s1) in sl:
            n = s1 - s0
            qTf[bb][:, s0:s1] = res.results[c]["qT"][:, o:o + n]
            ckvf[bb][:, s0:s1] = res.results[c]["ckvT"][:, o:o + n]
            vvf[bb][s0:s1, :] = res.results[c]["vO"][o:o + n, :]
            o += n
        qn[c] = res.results[c]["qn"][0]
        kn[c] = res.results[c]["kn"][0]
    v65 = np.concatenate([vvf.reshape(B, T, 16, 64),
                          np.ones((B, T, 16, 1), np.float32)], axis=-1).reshape(B, T, 16 * 65)
    wpT = np.ascontiguousarray(W['w_proj'].T)
    bp = np.ascontiguousarray(W['b_proj'].reshape(NCH, 128).T)
    out = np.empty_like(x)
    ctx = build_l0attn_b(0)
    maps = []
    for c in range(NCORES):
        bb, sl = core_slices(c, True)
        xs = np.concatenate([x[bb, s0:s1, :] for (s0, s1) in sl], axis=0)
        qs = np.concatenate([qTf[bb][:, s0:s1] for (s0, s1) in sl], axis=1)
        maps.append({"xT": np.ascontiguousarray(xs.T), "qT": np.ascontiguousarray(qs),
                     "ckvTf": ckvf[bb], "vf": v65[bb], "wpT": wpT, "bp_c": bp,
                     "qn_r": qn[c:c + 1], "kn_r": kn[bb * 4:(bb + 1) * 4].reshape(1, 4),
                     "c_onesc": np.ones((128, 1), np.float32),
                     "c_onesr": np.ones((1, 128), np.float32)})
    res = run_bass_kernel_spmd(ctx.nc, maps, core_ids=list(range(NCORES)))
    for c in range(NCORES):
        bb, sl = core_slices(c, True)
        o = 0
        for (s0, s1) in sl:
            n = s1 - s0
            out[bb, s0:s1, :] = res.results[c]["yT"][:, o:o + n].T
            o += n
    return out


def build_l3attn_a():
    ctx = Ctx("l3attn_a")
    nc = ctx.nc
    xT = ctx.din("xT", [C, NT])
    wqT = ctx.din("wqT", [C, C])
    wkT = ctx.din("wkT", [C, C])
    wvK = ctx.din("wvK", [C, C])
    c_onesr = ctx.din("c_onesr", [1, 128])
    c_hsel = ctx.din("c_hsel", [128, 2])
    qT = ctx.dout("qT", [C, NT])
    kT = ctx.dout("kT", [C, NT])
    vO = ctx.dout("vO", [NT, C])
    qn = ctx.dout("qn", [1, 16], F32)
    kn = ctx.dout("kn", [1, 16], F32)

    X = ctx.sb("X", [128, NCH, NT])
    QO = ctx.sb("QO", [128, NCH, NT])
    WV = ctx.sb("WV", [128, NCH, C])
    VS = ctx.sb("VS", [128, 2, C])
    ONESR = ctx.sb("ONESR", [1, 128])
    HSEL = ctx.sb("HSEL", [128, 2])
    SQB = ctx.sb("SQB", [128, 512])
    NALL = ctx.sb("NALL", [2, NCH * 4], F32)
    NMAXQ = ctx.sb("NMAXQ", [2, NCH], F32)
    NMAXK = ctx.sb("NMAXK", [2, NCH], F32)
    PBC = [ctx.ps(f"pbc{i}", [128, 512]) for i in range(2)]

    ld = ctx.sem("ld")
    st = ctx.sem("st")

    with nc.Block() as block:
        @block.sync
        def _(s):
            dma(s, ld, X[:, :, :], xT.rearrange("(c p) n -> p c n", p=128))
            dma(s, ld, ONESR[:, :], c_onesr[:, :])
            dma(s, ld, HSEL[:, :], c_hsel[:, :])
            dma(s, ld, WV[:, :, :], wvK.rearrange("(c p) m -> p c m", p=128))

        emit_projF(ctx, block, "q", X, wqT, QO, C, C, wait=(ld, ld.n))
        @block.sync
        def _(s):
            s.wait_ge(ctx.sem("qev").h, ctx.sem("qev").n)
            dma(s, st, qT.rearrange("(c p) n -> p c n", p=128), QO[:, :, :])
        emit_norms(ctx, block, "qn", QO, HSEL, SQB, NALL, NMAXQ,
                   (ctx.sem("qev"), ctx.sem("qev").n), PBC[0])
        @block.sync
        def _(s):
            s.wait_ge(ctx.sem("qnrd").h, ctx.sem("qnrd").n)
            st.inc(s.dma_start(out=qn.rearrange("a (c p) -> a p c", p=2), in_=NMAXQ[:, :],
                               allow_slow_non_contiguous=True), 16)
        emit_projF(ctx, block, "k", X, wkT, QO, C, C,
                   wait=[(st, st.n), (ctx.sem("qnrd"), ctx.sem("qnrd").n)])
        @block.sync
        def _(s):
            s.wait_ge(ctx.sem("kev").h, ctx.sem("kev").n)
            dma(s, st, kT.rearrange("(c p) n -> p c n", p=128), QO[:, :, :])
        emit_norms(ctx, block, "kn", QO, HSEL, SQB, NALL, NMAXK,
                   (ctx.sem("kev"), ctx.sem("kev").n), PBC[0])
        @block.sync
        def _(s):
            s.wait_ge(ctx.sem("knrd").h, ctx.sem("knrd").n)
            st.inc(s.dma_start(out=kn.rearrange("a (c p) -> a p c", p=2), in_=NMAXK[:, :],
                               allow_slow_non_contiguous=True), 16)

        vmm = ctx.sem("vmm")
        vev = ctx.sem("vev")
        stv = [ctx.sem(f"stv{i}") for i in range(4)]
        for tt in range(NT // 128):
            for half in range(2):
                vcol = slice(half * 512, (half + 1) * 512)
                pb = PBC[half]
                sv = stv[(2 * tt + half) % 4]

                @block.tensor
                def _(t, tt=tt, half=half, vcol=vcol, pb=pb):
                    if tt == 0 and half == 0:
                        t.wait_ge(ctx.sem("knrd").h, ctx.sem("knrd").n)
                    t.wait_ge(vev.h, max(0, vev.n - 1))
                    for k in range(NCH):
                        last = t.matmul(pb[:, :], X[:, k, tt * 128:(tt + 1) * 128],
                                        WV[:, k, vcol], start=(k == 0), stop=(k == NCH - 1))
                    vmm.inc(last)

                @block.vector
                def _(v, tt=tt, vcol=vcol, pb=pb, sv=sv):
                    v.wait_ge(vmm.h, vmm.n)
                    v.wait_ge(sv.h, sv.n)
                    vev.inc(v.tensor_copy(VS[:, tt % 2, vcol], pb[:, :]))

                @block.sync
                def _(s, tt=tt, vcol=vcol, sv=sv):
                    s.wait_ge(vev.h, vev.n)
                    dma(s, sv, vO[tt * 128:(tt + 1) * 128, vcol], VS[:, tt % 2, vcol])

        @block.sync
        def _(s):
            s.wait_ge(st.h, st.n)
            for sv in stv:
                s.wait_ge(sv.h, sv.n)

    return ctx


def build_l3attn_b(core):
    ctx = Ctx("l3attn_b")
    nc = ctx.nc
    b, slices = core_slices(core, True)
    SC = 0.125

    xT = ctx.din("xT", [C, NT])
    qT = ctx.din("qT", [C, NT])
    kTf = ctx.din("kTf", [C, T])
    vf = ctx.din("vf", [T, 16 * 65])
    woK = ctx.din("woK", [C, C])
    qn_r = ctx.din("qn_r", [1, 16], F32)
    kn_r = ctx.din("kn_r", [1, 64], F32)
    spos = ctx.din("spos", [2, T])
    qpos2 = ctx.din("qpos2", [2, NT])
    slp2 = ctx.din("slp2", [2, 16], F32)
    c_onesc = ctx.din("c_onesc", [128, 1])
    c_onesr = ctx.din("c_onesr", [1, 128])
    yT = ctx.dout("yT", [C, NT])

    X = ctx.sb("X", [128, NCH, NT])
    QT = ctx.sb("QT", [128, NCH, NT])
    YT = ctx.sb("YT", [128, NCH, NT], BF16)
    QA = ctx.sb("QA", [66, NT])
    KB = ctx.sb("KB", [128, 2, 128])
    VB = ctx.sb("VB", [128, 2, 65])
    EB = ctx.sb("EB", [128, 2, 512])
    ZB = EB[:, 1, 0:512]
    ZI = ctx.sb("ZI", [1, 512])
    POS2 = ctx.sb("POS2", [2, NT])
    SLP = ctx.sb("SLP", [2, 16], F32)
    ONESC = ctx.sb("ONESC", [128, 1])
    ONESR = ctx.sb("ONESR", [1, 128])
    SQ2 = ctx.sb("SQ2", [128, 2, 512])
    ROWS = ctx.sb("ROWS", [1, 4, 512])
    NROW = ctx.sb("NROW", [1, 64], F32)
    FR = ctx.sb("FR", [1, 4, 16], F32)
    SQN = ctx.sb("SQN", [1, 16], F32)
    ROW16 = ctx.sb("ROW16", [1, 16])
    MH = ctx.sb("MH", [128, 16], F32)

    SP = [ctx.ps(f"sp{i}", [128, 512]) for i in range(2)]
    SB2 = ctx.ps("sb2", [128, 1024])
    YP = [[ctx.ps(f"yp{qc}{hh}", [128, 512]) for hh in range(2)] for qc in range(2)]

    ld = ctx.sem("ld")
    kvp = [ctx.sem(f"kvp{i}") for i in range(4)]
    qkd = ctx.sem("qkd")
    exd = ctx.sem("exd")
    afd = ctx.sem("afd")
    avd = ctx.sem("avd")
    evd = ctx.sem("evd")
    qad = ctx.sem("qad")
    fin = ctx.sem("fin")
    st_ = ctx.sem("st")

    with nc.Block() as block:
        @block.sync
        def _(s):
            dma(s, ld, X[:, :, :], xT.rearrange("(c p) n -> p c n", p=128))
            dma(s, ld, QT[:, :, :], qT.rearrange("(c p) n -> p c n", p=128))
            dma(s, ld, ONESC[:, :], c_onesc[:, :])
            dma(s, ld, ONESR[:, :], c_onesr[:, :])
            dma(s, ld, FR[0:1, 0, :], qn_r[:, :])
            dma(s, ld, NROW[:, :], kn_r[:, :])
            dma(s, ld, POS2[:, :], qpos2[:, :])
            dma(s, ld, SLP[:, :], slp2[:, :])

        @block.vector
        def _(v):
            v.wait_ge(ld.h, ld.n)
            ctx.chain(v, "V", [
                lambda: v.tensor_tensor(FR[0:1, 1, :], NROW[0:1, 0:16], NROW[0:1, 16:32], Op.max),
                lambda: v.tensor_tensor(FR[0:1, 1, :], FR[0:1, 1, :], NROW[0:1, 32:48], Op.max),
                lambda: v.tensor_tensor(FR[0:1, 1, :], FR[0:1, 1, :], NROW[0:1, 48:64], Op.max),
                lambda: v.tensor_tensor(FR[0:1, 0, :], FR[0:1, 0, :], FR[0:1, 1, :], Op.mult),
            ])
            ctx.vflush(v, ctx.sem("m0a"))

        @block.scalar
        def _(sc):
            sc.wait_ge(ctx.sem("m0a").h, 1)
            sc.activation(out=SQN[0:1, :], in_=FR[0:1, 0, :], func=AF.Sqrt)
            ctx.aflush(sc, ctx.sem("m0b"))

        @block.vector
        def _(v):
            v.wait_ge(ctx.sem("m0b").h, 1)
            ctx.chain(v, "V", [
                lambda: v.tensor_scalar_mul(SQN[0:1, :], SQN[0:1, :], -SC),
                lambda: v.tensor_copy(ROW16[0:1, :], SQN[0:1, :]),
            ])
            ctx.vflush(v, ctx.sem("m0c"))

        @block.tensor
        def _(t):
            t.wait_ge(ctx.sem("m0c").h, 1)
            ctx.sem("m0d").inc(t.matmul(SB2[:, 0:16], ONESR[:, :], ROW16[0:1, :],
                                        start=True, stop=True))

        @block.vector
        def _(v):
            v.wait_ge(ctx.sem("m0d").h, 1)
            v.tensor_copy(MH[:, :], SB2[:, 0:16])
            ctx.vflush(v, ctx.sem("m0e"))

        avd_after = {}
        for qbi, (q0g, q1g) in enumerate(slices):
            NST = q1g // 128
            lim = [(q0g + 512) // 128, (q0g + 1024) // 128]
            for h in range(16):
                @block.vector
                def _(v, h=h, qbi=qbi):
                    v.wait_ge(ctx.sem("m0e").h, 1)
                    v.wait_ge(avd.h, avd.n)
                    ctx.chain(v, "V", [
                        lambda h=h: v.tensor_copy(QA[0:64, :], QT[(h % 2) * 64:(h % 2) * 64 + 64, h // 2, :]),
                        lambda h=h: v.tensor_scalar(QA[64:66, :], POS2[0:2, :], SLP[0:2, h:h + 1],
                                                    None, Op.mult),
                    ])
                    ctx.vflush(v, qad)

                for sti in range(NST):
                    kv = kvp[sti % 4]

                    @block.sync
                    def _(s, sti=sti, h=h, qbi=qbi, kv=kv):
                        key = (qbi, h, sti - 2)
                        if key in avd_after:
                            s.wait_ge(avd.h, avd_after[key])
                        elif sti - 2 < 0 and (qbi, h) != (0, 0):
                            s.wait_ge(avd.h, avd.n)
                        dma(s, kv, KB[0:64, sti % 2, :],
                            kTf[h * 64:(h + 1) * 64, sti * 128:(sti + 1) * 128])
                        dma(s, kv, KB[64:66, sti % 2, :], spos[:, sti * 128:(sti + 1) * 128])
                        dma(s, kv, VB[:, sti % 2, :],
                            vf[sti * 128:(sti + 1) * 128, h * 65:(h + 1) * 65])

                    for qc in range(2):
                        if sti >= lim[qc]:
                            continue
                        qcol = slice(qbi * 1024 + qc * 512, qbi * 1024 + (qc + 1) * 512)

                        @block.tensor
                        def _(t, sti=sti, qc=qc, h=h, qcol=qcol, kv=kv):
                            t.wait_ge(kv.h, kv.n)
                            t.wait_ge(qad.h, qad.n)
                            t.wait_ge(exd.h, exd.n)
                            if sti == 0:
                                t.wait_ge(evd.h, evd.n)
                            qkd.inc(t.matmul(SP[qc][:, :], KB[0:66, sti % 2, :],
                                             QA[0:66, qcol], start=True, stop=True))

                        @block.scalar
                        def _(sc, sti=sti, qc=qc, h=h):
                            sc.wait_ge(qkd.h, qkd.n)
                            sc.wait_ge(avd.h, avd.n)
                            sc.wait_ge(evd.h, evd.n)
                            exd.inc(sc.activation(out=EB[:, qc, :], in_=SP[qc][:, :],
                                                  func=AF.Exp, bias=MH[:, h:h + 1],
                                                  scale=SC))

                        diag = (sti >= lim[qc] - 4)
                        if diag:
                            @block.gpsimd
                            def _(g, sti=sti, qc=qc, q0g=q0g):
                                g.wait_ge(exd.h, exd.n)
                                afd.inc(g.affine_select(
                                    EB[:, qc, :], EB[:, qc, :],
                                    [[1, 512]], Op.is_ge, 0.0,
                                    base=q0g + qc * 512 - sti * 128,
                                    channel_multiplier=-1))

                        @block.tensor
                        def _(t, sti=sti, qc=qc, diag=diag):
                            if diag:
                                t.wait_ge(afd.h, afd.n)
                            else:
                                t.wait_ge(exd.h, exd.n)
                            if sti == 0:
                                t.wait_ge(evd.h, evd.n)
                            first = (sti == 0)
                            lastst = (sti == lim[qc] - 1)
                            avd.inc(t.matmul(YP[qc][0][0:65, :], VB[:, sti % 2, 0:65],
                                             EB[:, qc, 0:512], start=first, stop=lastst,
                                             skip_group_check=True))

                    avd_after[(qbi, h, sti)] = avd.n

                for qc in range(2):
                    qcol = slice(qbi * 1024 + qc * 512, qbi * 1024 + (qc + 1) * 512)

                    @block.vector
                    def _(v, qc=qc):
                        v.wait_ge(avd.h, avd.n)
                        with nc.allow_low_precision(reason="1/Z row"):
                            v.reciprocal(ZI[0:1, :], YP[qc][0][64:65, :])
                        ctx.vflush(v, ctx.sem("zi"))

                    @block.tensor
                    def _(t):
                        t.wait_ge(ctx.sem("zi").h, ctx.sem("zi").n)
                        ctx.sem("zb").inc(t.matmul(SB2[:, 0:512], ONESR[:, :], ZI[0:1, :],
                                                   start=True, stop=True))

                    @block.vector
                    def _(v, qc=qc, h=h, qcol=qcol):
                        v.wait_ge(ctx.sem("zb").h, ctx.sem("zb").n)
                        ctx.chain(v, "V", [
                            lambda: v.tensor_copy(ZB[:, :], SB2[:, 0:512]),
                            lambda qc=qc, h=h, qcol=qcol: v.tensor_tensor(
                                YT[(h % 2) * 64:(h % 2) * 64 + 64, h // 2, qcol],
                                YP[qc][0][0:64, :], ZB[0:64, :], Op.mult),
                        ])
                        ctx.vflush(v, evd)

        OWB = ctx.sb("owbuf", [128, NCH, 128], BF16)
        emit_projF(ctx, block, "o", YT, woK, X, C, C,
                   wait=(evd, evd.n), add_into=True, wbuf=OWB,
                   psums=[SB2[:, 0:512], SB2[:, 512:1024]])
        emit_ln(ctx, block, X, SQ2, ROWS, ONESC, ONESR,
                [SP[0], SP[1], YP[0][0], YP[0][1]],
                ctx.sem("oev"), ctx.sem("oev").n, fin)

        @block.sync
        def _(s):
            s.wait_ge(fin.h, fin.n)
            dma(s, st_, yT.rearrange("(c p) n -> p c n", p=128), X[:, :, :])
            s.wait_ge(st_.h, st_.n)

    return ctx


def _alibi_slopes16():
    import math
    start = 2.0 ** (-2.0 ** (-(math.log2(16) - 3)))
    return np.array([start * start ** i for i in range(16)], dtype=np.float32)


def _kron_place(m):
    W = np.zeros((C, C), np.float32)
    for i in range(16):
        for j in range(16):
            W[i * 64 + np.arange(64), j * 64 + np.arange(64)] = m[i, j]
    return W


def launch_l3attn(x, W_attn, v_tmp, proj_tmp):
    ctxa = build_l3attn_a()
    wqT = np.ascontiguousarray(W_attn[:, 0:C])
    wkT = np.ascontiguousarray(W_attn[:, C:2 * C])
    wvK = _kron_place(v_tmp)
    woK = np.ascontiguousarray(_kron_place(proj_tmp).T)
    hsel = np.zeros((128, 2), np.float32); hsel[0:64, 0] = 1; hsel[64:128, 1] = 1
    in_maps = []
    for c in range(NCORES):
        bb, sl = core_slices(c, True)
        xs = np.concatenate([x[bb, s0:s1, :] for (s0, s1) in sl], axis=0)
        in_maps.append({"xT": np.ascontiguousarray(xs.T), "wqT": wqT, "wkT": wkT, "wvK": wvK,
                        "c_onesr": np.ones((1, 128), np.float32), "c_hsel": hsel})
    res = run_bass_kernel_spmd(ctxa.nc, in_maps, core_ids=list(range(NCORES)))
    qTf = np.zeros((B, C, T), np.float32)
    kTf = np.zeros((B, C, T), np.float32)
    vvf = np.zeros((B, T, C), np.float32)
    qn = np.zeros((NCORES, 16), np.float32)
    kn = np.zeros((NCORES, 16), np.float32)
    for c in range(NCORES):
        bb, sl = core_slices(c, True)
        o = 0
        for (s0, s1) in sl:
            n = s1 - s0
            qTf[bb][:, s0:s1] = res.results[c]["qT"][:, o:o + n]
            kTf[bb][:, s0:s1] = res.results[c]["kT"][:, o:o + n]
            vvf[bb][s0:s1, :] = res.results[c]["vO"][o:o + n, :]
            o += n
        qn[c] = res.results[c]["qn"][0]
        kn[c] = res.results[c]["kn"][0]
    v65 = np.concatenate([vvf.reshape(B, T, 16, 64),
                          np.ones((B, T, 16, 1), np.float32)], axis=-1).reshape(B, T, 16 * 65)
    slopes = _alibi_slopes16()
    SC = 0.125
    spos = np.stack([np.arange(T, dtype=np.float32), np.ones(T, np.float32)])
    slp2 = np.stack([slopes / SC, slopes / SC]).astype(np.float32)
    out = np.empty_like(x)
    for j in range(4):
        ctx = build_l3attn_b(j)
        maps = []
        for b_ in range(B):
            c = b_ * 4 + j
            bb, sl = core_slices(c, True)
            xs = np.concatenate([x[bb, s0:s1, :] for (s0, s1) in sl], axis=0)
            qs = np.concatenate([qTf[bb][:, s0:s1] for (s0, s1) in sl], axis=1)
            toks = np.concatenate([np.arange(s0, s1) for (s0, s1) in sl]).astype(np.float32)
            qpos2 = np.stack([np.ones(NT, np.float32), -toks])
            maps.append({"xT": np.ascontiguousarray(xs.T), "qT": np.ascontiguousarray(qs),
                         "kTf": kTf[bb], "vf": v65[bb], "woK": woK,
                         "qn_r": qn[c:c + 1], "kn_r": kn[bb * 4:(bb + 1) * 4].reshape(1, 64),
                         "spos": spos, "qpos2": qpos2, "slp2": slp2,
                         "c_onesc": np.ones((128, 1), np.float32),
                         "c_onesr": np.ones((1, 128), np.float32)})
        res = run_bass_kernel_spmd(ctx.nc, maps, core_ids=[j, 4 + j])
        for b_ in range(B):
            c = b_ * 4 + j
            bb, sl = core_slices(c, True)
            o = 0
            for (s0, s1) in sl:
                n = s1 - s0
                out[bb, s0:s1, :] = res.results[b_]["yT"][:, o:o + n].T
                o += n
    return out


def build_l2ffn():
    ctx = Ctx("l2ffn")
    nc = ctx.nc
    NE, INTER = 8, 2048
    NB = 2
    NTB = NT // NB

    xT = ctx.din("xT", [C, NT])
    w1T = ctx.din("w1T", [NE, C, 2 * INTER])
    w2T = ctx.din("w2T", [NE, INTER, C])
    rwT = ctx.din("rwT", [C, NE])
    c_onesc = ctx.din("c_onesc", [128, 1])
    c_onesr = ctx.din("c_onesr", [1, 128])
    c_ident = ctx.din("c_ident", [128, 128], F32)
    crow_d = ctx.nc.dram_tensor("crow_d", [NE, NT], F32R, kind="Internal").ap()
    yT = ctx.dout("yT", [C, NT])

    X = ctx.sb("X", [128, NCH, NT])
    HE = ctx.sb("HE", [128, 16, NTB], BF16)
    ACC = ctx.sb("ACC", [128, NCH, NTB], F32)
    CB = ctx.sb("CB", [128, NTB])
    GT = ctx.sb("GT", [128, 16, 8], F32)
    PRB = ctx.sb("PRB", [128, 16, 8], F32)
    MSK = ctx.sb("MSK", [128, 16, 8], F32)
    MR = ctx.sb("MR", [128, 16], F32)
    ZR = ctx.sb("ZR", [128, 16], F32)
    CMB = ctx.sb("CMB", [128, 16, 8], F32)
    CST = ctx.sb("CST", [8, NT])
    CRR = ctx.sb("CRR", [1, NT])
    RW = ctx.sb("RW", [128, NCH, 8])
    IDNT = ctx.sb("IDNT", [128, 128], F32)
    W1G = [ctx.sb(f"w1g{i}", [128, NCH, 128]) for i in range(2)]
    W1U = [ctx.sb(f"w1u{i}", [128, NCH, 128]) for i in range(2)]
    W2B = ctx.sb("w2b", [128, 16, 128], BF16)
    SGB = ctx.sb("SGB", [128, 512], F32)
    HF = ctx.sb("HF", [128, 512], F32)
    ONESC = ctx.sb("ONESC", [128, 1])
    ONESR = ctx.sb("ONESR", [1, 128])
    SQ2 = ctx.sb("SQ2", [128, 2, 512])
    ROWS = ctx.sb("ROWS", [1, 4, 512])

    PS = [ctx.ps(f"ps{i}", [128, 512]) for i in range(4)]
    PT = ctx.ps("pt", [128, 128], F32)

    ld = ctx.sem("ld")
    st_ = ctx.sem("st")
    rt = ctx.sem("rt")
    rv = ctx.sem("rv")
    tp = ctx.sem("tp")
    cw = ctx.sem("cw")
    crl = ctx.sem("crl")
    cbd = ctx.sem("cbd")
    w1p = [ctx.sem("w1l0"), ctx.sem("w1l1")]
    w2l = ctx.sem("w2l")
    gmm = ctx.sem("gmm")
    sg = ctx.sem("sg")
    hrd = ctx.sem("hrd")
    dmm = ctx.sem("dmm")
    acd = ctx.sem("acd")
    fin = ctx.sem("fin")

    with nc.Block() as block:
        @block.sync
        def _(s):
            dma(s, ld, X[:, :, :], xT.rearrange("(c p) n -> p c n", p=128))
            dma(s, ld, RW[:, :, :], rwT.rearrange("(c p) m -> p c m", p=128))
            dma(s, ld, ONESC[:, :], c_onesc[:, :])
            dma(s, ld, ONESR[:, :], c_onesr[:, :])
            dma(s, ld, IDNT[:, :], c_ident[:, :])

        for tt in range(16):
            @block.tensor
            def _(t, tt=tt):
                if tt == 0:
                    t.wait_ge(ld.h, ld.n)
                t.wait_ge(rv.h, rv.n)
                for k in range(NCH):
                    last = t.matmul(PT[:, 0:8], X[:, k, tt * 128:(tt + 1) * 128],
                                    RW[:, k, :], start=(k == 0), stop=(k == NCH - 1))
                rt.inc(last)

            @block.vector
            def _(v, tt=tt):
                v.wait_ge(rt.h, rt.n)
                rv.inc(v.tensor_copy(GT[:, tt, :], PT[:, 0:8]))

        @block.scalar
        def _(sc):
            sc.wait_ge(rv.h, rv.n)
            sg.inc(sc.activation(out=PRB[:, :, :], in_=GT[:, :, :], func=AF.Exp))

        @block.vector
        def _(v):
            v.wait_ge(sg.h, sg.n)
            ops = [lambda: v.tensor_reduce(MR[:, :], GT[:, :, :], mybir.AxisListType.X, Op.max)]
            for tt in range(16):
                ops.append(lambda tt=tt: v.tensor_scalar(MSK[:, tt, :], GT[:, tt, :],
                                                         MR[:, tt:tt + 1], None, Op.is_equal))
                ops.append(lambda tt=tt: v.scalar_tensor_tensor(GT[:, tt, :], MSK[:, tt, :],
                                                                -1e30, GT[:, tt, :], Op.mult, Op.add))
            ops.append(lambda: v.tensor_reduce(ZR[:, :], GT[:, :, :], mybir.AxisListType.X, Op.max))
            for tt in range(16):
                ops.append(lambda tt=tt: v.tensor_scalar(GT[:, tt, :], GT[:, tt, :],
                                                         ZR[:, tt:tt + 1], None, Op.is_equal))
            ops.append(lambda: v.tensor_tensor(MSK[:, :, :], MSK[:, :, :], GT[:, :, :], Op.add))
            ops.append(lambda: v.tensor_tensor(CMB[:, :, :], PRB[:, :, :], MSK[:, :, :], Op.mult))
            ops.append(lambda: v.tensor_reduce(ZR[:, :], PRB[:, :, :], mybir.AxisListType.X, Op.add))
            ops.append(lambda: v.reciprocal(ZR[:, :], ZR[:, :]))
            for tt in range(16):
                ops.append(lambda tt=tt: v.tensor_scalar(CMB[:, tt, :], CMB[:, tt, :],
                                                         ZR[:, tt:tt + 1], None, Op.mult))
            with nc.allow_low_precision(reason="router recip"):
                ctx.chain(v, "V", ops)
            ctx.vflush(v, rv)

        for tt in range(16):
            @block.tensor
            def _(t, tt=tt):
                t.wait_ge(rv.h, rv.n)
                t.wait_ge(cw.h, cw.n)
                tp.inc(t.transpose(PT[0:8, 0:128], CMB[:, tt, :].bitcast(F32) if False else CMB[:, tt, :], IDNT))

            @block.vector
            def _(v, tt=tt):
                v.wait_ge(tp.h, tp.n)
                cw.inc(v.tensor_copy(CST[0:8, tt * 128:(tt + 1) * 128], PT[0:8, 0:128]))

        @block.sync
        def _(s):
            s.wait_ge(cw.h, cw.n)
            dma(s, crl, crow_d[:, :], CST[0:8, :].bitcast(F32R))

        w1_after = {}
        for nb in range(NB):
            ncols = slice(nb * NTB, (nb + 1) * NTB)
            for e in range(NE):
                @block.sync
                def _(s, e=e):
                    s.wait_ge(crl.h, 16)
                    s.wait_ge(cbd.h, cbd.n)
                    dma(s, crl, CRR[0:1, :], crow_d[e:e + 1, :])

                @block.tensor
                def _(t, e=e, nb=nb):
                    t.wait_ge(crl.h, crl.n)
                    t.wait_ge(hrd.h, hrd.n)
                    t.wait_ge(acd.h, acd.n)
                    ctx.sem("cbm").inc(t.matmul(PS[3][:, :], ONESR[:, :],
                                                CRR[0:1, nb * NTB:nb * NTB + 512],
                                                start=True, stop=True))
                    ctx.sem("cbm").inc(t.matmul(PT[:, 0:128].bitcast(F32) if False else PS[2][:, :],
                                                ONESR[:, :], CRR[0:1, nb * NTB + 512:nb * NTB + 1024],
                                                start=True, stop=True))

                @block.vector
                def _(v, e=e):
                    v.wait_ge(ctx.sem("cbm").h, ctx.sem("cbm").n)
                    v.tensor_copy(CB[:, 0:512], PS[3][:, :])
                    v.tensor_copy(CB[:, 512:1024], PS[2][:, :])
                    ctx.vflush(v, cbd)

                for m in range(16):
                    buf = m % 2

                    @block.sync
                    def _(s, e=e, m=m, nb=nb, buf=buf):
                        key = (nb, e, m - 2)
                        if key in w1_after:
                            s.wait_ge(gmm.h, w1_after[key])
                        elif m - 2 < 0 and (nb, e) != (0, 0):
                            s.wait_ge(gmm.h, gmm.n)
                        dma(s, w1p[buf], W1G[buf][:, :, :],
                            w1T[e, :, m * 128:(m + 1) * 128].rearrange("(c p) m -> p c m", p=128))
                        dma(s, w1p[buf], W1U[buf][:, :, :],
                            w1T[e, :, INTER + m * 128:INTER + (m + 1) * 128].rearrange("(c p) m -> p c m", p=128))

                    for q2 in range(2):
                        qq = slice(nb * NTB + q2 * 512, nb * NTB + (q2 + 1) * 512)
                        hq = slice(q2 * 512, (q2 + 1) * 512)

                        @block.tensor
                        def _(t, e=e, m=m, buf=buf, qq=qq, q2=q2):
                            t.wait_ge(w1p[buf].h, w1p[buf].n)
                            t.wait_ge(sg.h, sg.n)
                            t.wait_ge(hrd.h, hrd.n)
                            for k in range(NCH):
                                last = t.matmul(PS[0][:, :], W1G[buf][:, k, :],
                                                X[:, k, qq], start=(k == 0), stop=(k == NCH - 1))
                            gmm.inc(last)
                            for k in range(NCH):
                                last = t.matmul(PS[1][:, :], W1U[buf][:, k, :],
                                                X[:, k, qq], start=(k == 0), stop=(k == NCH - 1))
                            gmm.inc(last)

                        @block.scalar
                        def _(sc, m=m, q2=q2):
                            sc.wait_ge(gmm.h, gmm.n)
                            sg.inc(sc.activation(out=SGB[:, :], in_=PS[0][:, :], func=AF.Silu))

                        @block.vector
                        def _(v, m=m, hq=hq, q2=q2):
                            v.wait_ge(sg.h, sg.n)
                            v.wait_ge(cbd.h, cbd.n)
                            ctx.chain(v, "V", [
                                lambda m=m, hq=hq: v.tensor_tensor(HF[:, :], PS[1][:, :],
                                                                   SGB[:, :], Op.mult),
                                lambda m=m, hq=hq, q2=q2: v.tensor_tensor(
                                    HE[:, m, hq], HF[:, :], CB[:, q2 * 512:(q2 + 1) * 512], Op.mult),
                            ])
                            ctx.vflush(v, hrd)

                    w1_after[(nb, e, m)] = gmm.n

                for oc in range(NCH):
                    @block.gpsimd
                    def _(g, e=e, oc=oc):
                        g.wait_ge(dmm.h, dmm.n)
                        dma(g, w2l, W2B[:, :, :],
                            w2T[e, :, oc * 128:(oc + 1) * 128].rearrange("(c p) m -> p c m", p=128))

                    for q2 in range(2):
                        hq = slice(q2 * 512, (q2 + 1) * 512)
                        aq = slice(q2 * 512, (q2 + 1) * 512)

                        @block.tensor
                        def _(t, e=e, oc=oc, hq=hq, q2=q2):
                            if q2 == 0:
                                t.wait_ge(w2l.h, w2l.n)
                                t.wait_ge(hrd.h, hrd.n)
                            t.wait_ge(acd.h, acd.n)
                            for k in range(16):
                                last = t.matmul(PS[2][:, :], W2B[:, k, :],
                                                HE[:, k, hq], start=(k == 0), stop=(k == 15))
                            dmm.inc(last)

                        @block.vector
                        def _(v, e=e, oc=oc, aq=aq):
                            v.wait_ge(dmm.h, dmm.n)
                            if e == 0:
                                acd.inc(v.tensor_copy(ACC[:, oc, aq], PS[2][:, :]))
                            else:
                                acd.inc(v.tensor_tensor(ACC[:, oc, aq], ACC[:, oc, aq],
                                                        PS[2][:, :], Op.add))

            @block.vector
            def _(v, nb=nb, ncols=ncols):
                v.wait_ge(acd.h, acd.n)
                for oc in range(NCH):
                    for q2 in range(2):
                        cc = slice(nb * NTB + q2 * 512, nb * NTB + (q2 + 1) * 512)
                        v.tensor_tensor(X[:, oc, cc], X[:, oc, cc],
                                        ACC[:, oc, q2 * 512:(q2 + 1) * 512], Op.add)
                ctx.vflush(v, ctx.sem("resd"))

        emit_ln(ctx, block, X, SQ2, ROWS, ONESC, ONESR,
                [PS[0], PS[1], PS[2], PS[3]],
                ctx.sem("resd"), ctx.sem("resd").n, fin)

        @block.sync
        def _(s):
            s.wait_ge(fin.h, fin.n)
            dma(s, st_, yT.rearrange("(c p) n -> p c n", p=128), X[:, :, :])
            s.wait_ge(st_.h, st_.n)

    return ctx


def launch_l2ffn(x, w1, w2, router_w):
    ctx = build_l2ffn()
    w1T = np.ascontiguousarray(w1.transpose(0, 2, 1))
    w2T = np.ascontiguousarray(w2.transpose(0, 2, 1))
    rwT = np.ascontiguousarray(router_w.T)
    in_maps = []
    for c in range(NCORES):
        bb, sl = core_slices(c, True)
        xs = np.concatenate([x[bb, s0:s1, :] for (s0, s1) in sl], axis=0)
        in_maps.append({"xT": np.ascontiguousarray(xs.T), "w1T": w1T, "w2T": w2T, "rwT": rwT,
                        "c_onesc": np.ones((128, 1), np.float32),
                        "c_onesr": np.ones((1, 128), np.float32),
                        "c_ident": np.eye(128, dtype=np.float32)})
    res = run_bass_kernel_spmd(ctx.nc, in_maps, core_ids=list(range(NCORES)))
    out = np.empty_like(x)
    for c in range(NCORES):
        bb, sl = core_slices(c, True)
        o = 0
        for (s0, s1) in sl:
            n = s1 - s0
            out[bb, s0:s1, :] = res.results[c]["yT"][:, o:o + n].T
            o += n
    return out


def emit_ln2(ctx, block, pfx, SRC_cols, OUT, GCOL, BCOL, ONESC, ONESR, PS4,
             wait, sem_out, shifts=None, ntok=512, eps=1e-5, SCR3=None, OUT_cols=None):
    if OUT_cols is None:
        OUT_cols = lambda cch, q: OUT[:, cch, q * 512:(q + 1) * 512]
    """LN over features with gamma/beta: OUT[:, c, :] = LN(SRC)(c)*g + b.
    SRC_cols: callable c -> AP [128, ntok] (allows shifted views).
    GCOL/BCOL: [128, NCH] F32. PS4: psA, psB, mB, rB."""
    psA, psB, mB, rB = PS4
    sq = ctx.sem(pfx + "sq")
    su = ctx.sem(pfx + "su")
    sm = ctx.sem(pfx + "sm")
    sr = ctx.sem(pfx + "sr")
    sb = ctx.sem(pfx + "sb")
    sa = ctx.sem(pfx + "sa")
    NQ = ntok // 512
    SCR = SCR3 if SCR3 is not None else OUT
    for q in range(NQ):
        @block.scalar
        def _(sc, q=q):
            for cch in range(NCH):
                if q == 0 and cch == 0:
                    sc.wait_ge(wait[0].h, wait[1])
                sc.wait_ge(su.h, max(0, su.n + cch - 1))
                sq.inc(sc.activation(out=SCR[:, cch % 2, q * 512:(q + 1) * 512],
                                     in_=SRC_cols(cch, q), func=AF.Square))

        @block.tensor
        def _(t, q=q):
            if q == 0:
                t.wait_ge(wait[0].h, wait[1])
            t.wait_ge(sa.h, sa.n)
            for cch in range(NCH):
                last = t.matmul(psA[0:1, :], ONESC, SRC_cols(cch, q),
                                start=(cch == 0), stop=(cch == NCH - 1))
            sm.inc(last)
            for cch in range(NCH):
                t.wait_ge(sq.h, sq.n - (NCH - 1) + cch)
                su.inc(t.matmul(psB[0:1, :], ONESC,
                                SCR[:, cch % 2, q * 512:(q + 1) * 512],
                                start=(cch == 0), stop=(cch == NCH - 1)))

        ROWS = ctx.sb(pfx + f"rw{q}", [1, 4, 512]) if False else ctx._ln2rows
        MEAN = ROWS[0:1, 0, :]
        RSTD = ROWS[0:1, 1, :]

        @block.vector
        def _(v, q=q):
            v.wait_ge(sm.h, sm.n)
            v.wait_ge(su.h, su.n)
            ctx.chain(v, "V", [
                lambda: v.tensor_scalar_mul(MEAN, psA[0:1, :], 1.0 / C),
                lambda: v.tensor_scalar(RSTD, psB[0:1, :], 1.0 / C, eps, Op.mult, Op.add),
                lambda: v.tensor_tensor(ROWS[0:1, 2, :], MEAN, MEAN, Op.mult),
                lambda: v.tensor_tensor(RSTD, RSTD, ROWS[0:1, 2, :], Op.subtract),
            ])
            ctx.vflush(v, sr)

        @block.scalar
        def _(sc, q=q):
            sc.wait_ge(sr.h, sr.n)
            sc.activation(out=ROWS[0:1, 3, :], in_=RSTD, func=AF.Sqrt)
            ctx.aflush(sc, sr)

        @block.vector
        def _(v, q=q):
            v.wait_ge(sr.h, sr.n)
            with ctx.nc.allow_low_precision(reason="rstd"):
                v.reciprocal(RSTD, ROWS[0:1, 3, :])
            ctx.vflush(v, sr)

        @block.tensor
        def _(t, q=q):
            t.wait_ge(sr.h, sr.n)
            sb.inc(t.matmul(mB[:, :], ONESR, MEAN, start=True, stop=True))
            sb.inc(t.matmul(rB[:, :], ONESR, RSTD, start=True, stop=True))

        @block.vector
        def _(v, q=q):
            v.wait_ge(sb.h, sb.n)
            for cch in range(NCH):
                ctx.chain(v, "V", [
                    lambda cch=cch: v.tensor_tensor(OUT_cols(cch, q),
                                                    SRC_cols(cch, q), mB[:, :], Op.subtract),
                    lambda cch=cch: v.tensor_tensor(OUT_cols(cch, q),
                                                    OUT_cols(cch, q), rB[:, :], Op.mult),
                ] + ([] if GCOL is None else [
                    lambda cch=cch: v.tensor_scalar(OUT_cols(cch, q),
                                                    OUT_cols(cch, q),
                                                    GCOL[:, cch:cch + 1], BCOL[:, cch:cch + 1],
                                                    Op.mult, Op.add)]))
            ctx.vflush(v, sa)

    @block.vector
    def _(v):
        v.wait_ge(sa.h, sa.n)
        sem_out.inc(v.tensor_copy(ctx._ln2rows[0:1, 2, 0:1], ctx._ln2rows[0:1, 2, 0:1]))


def build_lffn_inner(core, first_layer_of_seq, wrap, phase='both'):
    ctx = Ctx("lffn")
    nc = ctx.nc
    b, slices = core_slices(core, True)
    REG = 1538
    OWN = 514
    K0 = OWN - 512
    QKS = 8.0

    xh = ctx.din("xh", [C, 2 * REG])
    wqkvT = ctx.din("wqkvT", [C, 1536])
    woT = ctx.din("woT", [512, C])
    w1aT = ctx.din("w1aT", [C, 2730])
    w1gT = ctx.din("w1gT", [C, 2730])
    w2T = ctx.din("w2T", [2730, C])
    qsr = ctx.din("qsr", [1, 128], F32)
    qs2 = ctx.din("qs2", [128, 2], F32)
    b1a_c = ctx.din("b1a_c", [128, 22], F32)
    b1g_c = ctx.din("b1g_c", [128, 22], F32)
    b2_c = ctx.din("b2_c", [128, NCH], F32)
    lng_c = ctx.din("lng_c", [128, NCH], F32)
    lnb_c = ctx.din("lnb_c", [128, NCH], F32)
    csq = ctx.din("csq", [128, 2 * REG], F32)
    snq = ctx.din("snq", [128, 2 * REG], F32)
    csk = ctx.din("csk", [128, 2 * REG], F32)
    snk = ctx.din("snk", [128, 2 * REG], F32)
    c_onesc = ctx.din("c_onesc", [128, 1])
    c_onesr = ctx.din("c_onesr", [1, 128])
    c_hselt = ctx.din("c_hselt", [2, 128])
    c_hsel = ctx.din("c_hsel", [128, 2])
    c_vones = ctx.din("c_vones", [128, 96])
    lbm = ctx.din("lbm", [128, 1], F32)
    if wrap:
        xin0 = ctx.din("xin0", [C, NT])
    yT = ctx.dout("yT", [C, NT])

    XH = ctx.sb("XH", [128, NCH, REG])
    if phase != 'ffn':
        QF = ctx.sb("QF", [128, 4, 1024])
        KF = ctx.sb("KF", [128, 4, 1536])
        VT = ctx.sb("VT", [128, 12, 8, 65])
        WQV = ctx.sb("WQV", [128, NCH, 512])
        TBL = ctx.sb("TBL", [128, 2, 512], F32)
        YW = ctx.sb("YW", [128, 4, 1024], BF16)
    if phase != 'attn':
        HLN = ctx.sb("HLN", [128, NCH, 1024], BF16)
        LNSCR = ctx.sb("LNSCR", [128, 2, 1024])
        AH = ctx.sb("AH", [128, 22, 512], BF16)
    EB = ctx.sb("EB", [128, 1024])
    SC64 = ctx.sb("SC64", [128, 512])
    RH = ctx.sb("RH", [128, 512])
    ZB = RH

    WB1 = ctx.sb("wb1", [128, NCH, 128])
    WF1A = ctx.sb("wf1a", [128, NCH, 128], BF16)
    WF1G = ctx.sb("wf1g", [128, NCH, 128], BF16)
    WB2 = ctx.sb("wb2", [128, 22, 128], BF16)
    QS = ctx.sb("QS", [128, 2], F32)
    B1A = ctx.sb("B1A", [128, 22], F32)
    B1G = ctx.sb("B1G", [128, 22], F32)
    B2C = ctx.sb("B2C", [128, NCH], F32)
    LNG = ctx.sb("LNG", [128, NCH], F32)
    LNB = ctx.sb("LNB", [128, NCH], F32)
    MB = ctx.sb("MB", [128, 8], F32)
    RR = ctx.sb("RR", [2, 512])
    RB2 = ctx.sb("RB2", [2, 512], F32)
    FR = ctx.sb("FR", [1, 3, 64], F32)
    ROW8 = ctx.sb("ROW8", [1, 8])
    ONESC = ctx.sb("ONESC", [128, 1])
    ONESR = ctx.sb("ONESR", [1, 128])
    HSELT = ctx.sb("HSELT", [2, 128])
    HSEL = ctx.sb("HSEL", [128, 2])
    LBM = ctx.sb("LBM", [128, 1], F32)
    ctx._ln2rows = ctx.sb("LN2R", [1, 4, 512])
    ZI = ctx._ln2rows[0:1, 3, :]

    PS = [ctx.ps(f"ps{i}", [128, 512]) for i in range(4)]
    SPP = ctx.ps("spp", [128, 1024])
    YPP = [ctx.ps(f"ypp{i}", [128, 512]) for i in range(2)]

    ld = ctx.sem("ld")
    xld = ctx.sem("xld")
    tld = ctx.sem("tld")
    st_ = ctx.sem("st")
    fin = ctx.sem("fin")
    pj = ctx.sem("pj")
    pv = ctx.sem("pv")
    wqv = ctx.sem("wqv")
    nrm = ctx.sem("nrm")
    nrv = ctx.sem("nrv")
    rot = ctx.sem("rot")
    qkd = ctx.sem("qkd")
    exd = ctx.sem("exd")
    afd = ctx.sem("afd")
    avd = ctx.sem("avd")
    evd = ctx.sem("evd")
    w2s = ctx.sem("w2s")
    gmm = ctx.sem("gmm")
    gel_s = ctx.sem("gel")
    ahd = ctx.sem("ahd")
    dmm = ctx.sem("dmm")
    xup = ctx.sem("xup")

    def xsh(k, col0, w=512):
        off = col0 - (1 if k >= 4 else 0)
        return XH[:, k, off:off + w]

    with nc.Block() as block:
        @block.sync
        def _(s):
            dma(s, ld, ONESC[:, :], c_onesc[:, :])
            dma(s, ld, ONESR[:, :], c_onesr[:, :])
            dma(s, ld, HSELT[:, :], c_hselt[:, :])
            dma(s, ld, HSEL[:, :], c_hsel[:, :])
            dma(s, ld, QS[:, :], qs2[:, :])
            dma(s, ld, B1A[:, :], b1a_c[:, :])
            dma(s, ld, B1G[:, :], b1g_c[:, :])
            dma(s, ld, B2C[:, :], b2_c[:, :])
            dma(s, ld, LNG[:, :], lng_c[:, :])
            dma(s, ld, LNB[:, :], lnb_c[:, :])
            dma(s, ld, FR[0:1, 0, :], qsr[0:1, 0:64])
            dma(s, ld, FR[0:1, 1, :], qsr[0:1, 64:128])
            dma(s, ld, LBM[:, :], lbm[:, :])
            if phase != 'ffn':
                dma(s, ld, WQV[:, :, :], wqkvT[:, 1024:1536].rearrange("(c p) m -> p c m", p=128))

        if phase != 'ffn':
            @block.gpsimd
            def _(g):
                ctx.sem("vto").inc(g.dma_start(
                    out=VT[:, :, :, 64:65],
                    in_=c_vones.rearrange("p (t h) -> p t h", t=12),
                    allow_slow_non_contiguous=True), 16)

            @block.vector
            def _(v):
                v.wait_ge(ld.h, ld.n)
                ctx.chain(v, "V", [
                    lambda: v.tensor_reduce(FR[0:1, 2, 0:1], FR[0:1, 0, :], mybir.AxisListType.X,
                                            Op.max, apply_absolute_value=True),
                    lambda: v.tensor_reduce(FR[0:1, 2, 1:2], FR[0:1, 1, :], mybir.AxisListType.X,
                                            Op.max, apply_absolute_value=True),
                    lambda: v.tensor_tensor(FR[0:1, 2, 2:3], FR[0:1, 2, 0:1], FR[0:1, 2, 1:2], Op.mult),
                    lambda: v.tensor_scalar_mul(FR[0:1, 2, 2:3], FR[0:1, 2, 2:3], -QKS),
                    lambda: v.tensor_scalar(ROW8[0:1, :], ONESR[0:1, 0:8], FR[0:1, 2, 2:3], None, Op.mult),
                ])
                ctx.vflush(v, ctx.sem("m0c"))

            @block.tensor
            def _(t):
                t.wait_ge(ctx.sem("m0c").h, 1)
                ctx.sem("m0d").inc(t.matmul(SPP[:, 0:8], ONESR[:, :], ROW8[0:1, :],
                                            start=True, stop=True))

            @block.vector
            def _(v):
                v.wait_ge(ctx.sem("m0d").h, 1)
                v.tensor_copy(MB[:, :], SPP[:, 0:8])
                ctx.vflush(v, ctx.sem("m0e"))

        for ci in range(2):
            roff = ci * REG

            @block.sync
            def _(s, ci=ci, roff=roff):
                if ci == 1:
                    s.wait_ge(fin.h, fin.n)
                dma(s, xld, XH[:, :, :], xh[:, roff:roff + REG].rearrange("(c p) n -> p c n", p=128))

            for part, OUTB, ncs, base, wofs in (((("k", KF, 3, K0, 512), ("q", QF, 2, OWN, 0))) if phase != 'ffn' else ()):
                for oc in range(4):
                    @block.sync
                    def _(s, oc=oc, wofs=wofs):
                        s.wait_ge(pj.h, pj.n)
                        dma(s, wqv, WB1[:, :, :],
                            wqkvT[:, wofs + oc * 128:wofs + (oc + 1) * 128].rearrange("(c p) m -> p c m", p=128))

                    for nq in range(ncs):
                        @block.tensor
                        def _(t, oc=oc, nq=nq, base=base, ci=ci):
                            t.wait_ge(wqv.h, wqv.n)
                            if oc == 0 and nq == 0:
                                t.wait_ge(xld.h, xld.n)
                                t.wait_ge(rot.h, rot.n)
                                t.wait_ge(avd.h, avd.n)
                            t.wait_ge(pv.h, max(0, pv.n - 1))
                            for k in range(NCH):
                                last = t.matmul(PS[nq % 2][:, :], WB1[:, k, :],
                                                xsh(k, base + nq * 512), start=(k == 0), stop=(k == NCH - 1))
                            pj.inc(last)

                        @block.vector
                        def _(v, oc=oc, nq=nq, OUTB=OUTB):
                            v.wait_ge(pj.h, pj.n)
                            pv.inc(v.tensor_copy(OUTB[:, oc, nq * 512:(nq + 1) * 512], PS[nq % 2][:, :]))

            for tt in (range(12) if phase != 'ffn' else ()):
                @block.tensor
                def _(t, tt=tt, ci=ci):
                    if tt == 0:
                        t.wait_ge(ctx.sem("vto").h, 16)
                        t.wait_ge(pv.h, pv.n)
                        t.wait_ge(avd.h, avd.n)
                    t.wait_ge(pv.h, max(0, pv.n - 1))
                    for k in range(NCH):
                        last = t.matmul(PS[2 + tt % 2][:, :], xsh(k, K0 + tt * 128, 128), WQV[:, k, :],
                                        start=(k == 0), stop=(k == NCH - 1))
                    pj.inc(last)

                @block.vector
                def _(v, tt=tt):
                    v.wait_ge(pj.h, pj.n)
                    pv.inc(v.tensor_copy(VT[:, tt, :, 0:64],
                                         PS[2 + tt % 2].rearrange("p (h c) -> p h c", h=8)))

            for part, OUTB, width, tb0 in ((("q", QF, 1024, OWN), ("k", KF, 1536, 8)) if phase != 'ffn' else ()):
                scol = 0 if part == "q" else 1
                for oc in range(4):
                    for nq in range(width // 512):
                        @block.scalar
                        def _(sc, oc=oc, nq=nq, OUTB=OUTB):
                            if oc == 0 and nq == 0:
                                sc.wait_ge(pv.h, pv.n)
                            sc.wait_ge(nrm.h, nrm.n)
                            ctx.sem("nsq").inc(sc.activation(out=SC64[:, 0:512],
                                                             in_=OUTB[:, oc, nq * 512:(nq + 1) * 512],
                                                             func=AF.Square))

                        @block.tensor
                        def _(t, oc=oc, nq=nq):
                            t.wait_ge(ctx.sem("nsq").h, ctx.sem("nsq").n)
                            t.wait_ge(nrv.h, nrv.n)
                            nrm.inc(t.matmul(PS[3][0:2, :], HSEL[:, :], SC64[:, 0:512],
                                             start=True, stop=True))

                        @block.vector
                        def _(v, oc=oc, nq=nq):
                            v.wait_ge(nrm.h, nrm.n)
                            nrv.inc(v.tensor_copy(RR[0:2, nq * 512:(nq + 1) * 512] if oc == 99 else
                                                  RR[0:2, 0:512], PS[3][0:2, :]))

                        @block.scalar
                        def _(sc, oc=oc, nq=nq):
                            sc.wait_ge(nrv.h, nrv.n)
                            sc.activation(out=RB2[0:2, 0:512], in_=RR[0:2, 0:512], func=AF.Sqrt)
                            ctx.aflush(sc, ctx.sem("nsr"))

                        @block.vector
                        def _(v, oc=oc, nq=nq):
                            v.wait_ge(ctx.sem("nsr").h, ctx.sem("nsr").n)
                            with nc.allow_low_precision(reason="l2n"):
                                ctx.chain(v, "V", [
                                    lambda: v.tensor_scalar(RB2[0:2, 0:512], RB2[0:2, 0:512],
                                                            1e-12, None, Op.max),
                                    lambda: v.reciprocal(RR[0:2, 0:512], RB2[0:2, 0:512]),
                                ])
                            ctx.vflush(v, ctx.sem("nsv"))

                        @block.tensor
                        def _(t, oc=oc, nq=nq):
                            t.wait_ge(ctx.sem("nsv").h, ctx.sem("nsv").n)
                            ctx.sem("nsb").inc(t.matmul(PS[3][:, :], HSELT[:, :], RR[0:2, 0:512],
                                                        start=True, stop=True))

                        @block.vector
                        def _(v, oc=oc, nq=nq, OUTB=OUTB, scol=scol):
                            v.wait_ge(ctx.sem("nsb").h, ctx.sem("nsb").n)
                            nrv.inc(v.scalar_tensor_tensor(OUTB[:, oc, nq * 512:(nq + 1) * 512],
                                                           OUTB[:, oc, nq * 512:(nq + 1) * 512],
                                                           QS[:, scol:scol + 1], PS[3][:, :],
                                                           Op.mult, Op.mult))

            for part, OUTB, width, cstab, sntab in ((("q", QF, 1024, csq, snq), ("k", KF, 1536, csk, snk)) if phase != 'ffn' else ()):
                toff = OWN if part == "q" else K0

                for pq in range(width // 512):
                    @block.sync
                    def _(s, cstab=cstab, sntab=sntab, roff=roff, toff=toff, pq=pq):
                        s.wait_ge(ctx.sem("rtu").h, ctx.sem("rtu").n)
                        dma(s, tld, TBL[:, 0, :], cstab[:, roff + toff + pq * 512:roff + toff + (pq + 1) * 512])
                        dma(s, tld, TBL[:, 1, :], sntab[:, roff + toff + pq * 512:roff + toff + (pq + 1) * 512])

                    for oc in range(4):
                        @block.vector
                        def _(v, oc=oc, OUTB=OUTB, pq=pq):
                            v.wait_ge(nrv.h, nrv.n)
                            v.wait_ge(tld.h, tld.n)
                            cc = slice(pq * 512, (pq + 1) * 512)
                            CS = TBL[:, 0, :]
                            SN = TBL[:, 1, :]
                            ctx.chain(v, "V", [
                                lambda: v.tensor_scalar_mul(RH[0:32, :], OUTB[32:64, oc, cc], -1.0),
                                lambda: v.tensor_copy(RH[32:64, :], OUTB[0:32, oc, cc]),
                                lambda: v.tensor_copy(SC64[0:64, :], OUTB[64:128, oc, cc]),
                                lambda: v.tensor_scalar_mul(RH[64:96, :], SC64[32:64, :], -1.0),
                                lambda: v.tensor_copy(RH[96:128, :], SC64[0:32, :]),
                                lambda: v.tensor_tensor(OUTB[:, oc, cc], OUTB[:, oc, cc], CS, Op.mult),
                                lambda: v.tensor_tensor(RH[:, :], RH[:, :], SN, Op.mult),
                                lambda: v.tensor_tensor(OUTB[:, oc, cc], OUTB[:, oc, cc], RH[:, :], Op.add),
                            ])
                    @block.vector
                    def _(v):
                        ctx.vflush(v, ctx.sem("rtu"))
                @block.vector
                def _(v):
                    ctx.vflush(v, rot)

            if phase != 'ffn' and ci == 0:
                @block.vector
                def _(v):
                    for p_ in range(4):
                        ctx.sem("lbk").inc(v.tensor_scalar(
                            KF[:, p_, 0:512], KF[:, p_, 0:512], LBM[:, 0:1],
                            None, Op.mult))

            for w in (range(2) if phase != 'ffn' else ()):
                for pair in range(4):
                    st_range = range(8)
                    for sti in st_range:
                        @block.tensor
                        def _(t, w=w, pair=pair, sti=sti):
                            if pair == 0 and sti == st_range.start and w == 0:
                                t.wait_ge(ctx.sem("m0e").h, 1)
                                t.wait_ge(rot.h, rot.n)
                                t.wait_ge(ctx.sem("lbk").h, ctx.sem("lbk").n)
                            t.wait_ge(exd.h, exd.n)
                            if sti == st_range.start:
                                t.wait_ge(evd.h, evd.n)
                            kcol = w * 512 + sti * 128
                            qcol = w * 512
                            qkd.inc(t.matmul(SPP[:, 0:512], KF[0:64, pair, kcol:kcol + 128],
                                             QF[0:64, pair, qcol:qcol + 512], start=True, stop=True,
                                             tile_position=(0, 0)))
                            qkd.inc(t.matmul(SPP[:, 512:1024], KF[64:128, pair, kcol:kcol + 128],
                                             QF[64:128, pair, qcol:qcol + 512], start=True, stop=True,
                                             tile_position=(64, 0)))

                        @block.scalar
                        def _(sc, w=w, pair=pair, sti=sti):
                            sc.wait_ge(qkd.h, qkd.n)
                            sc.wait_ge(avd.h, avd.n)
                            sc.wait_ge(evd.h, evd.n)
                            exd.inc(sc.activation(out=EB[:, :], in_=SPP[:, :],
                                                  func=AF.Exp, bias=MB[:, 0:1], scale=QKS))

                        @block.gpsimd
                        def _(g, w=w, sti=sti):
                            g.wait_ge(exd.h, exd.n)
                            if sti >= 4:
                                afd.inc(g.affine_select(
                                    EB[:, :].rearrange("p (h n) -> p h n", h=2),
                                    EB[:, :].rearrange("p (h n) -> p h n", h=2),
                                    [[0, 2], [1, 512]], Op.is_ge, 0.0,
                                    base=512 - sti * 128, channel_multiplier=-1))
                            else:
                                afd.inc(g.affine_select(
                                    EB[:, :].rearrange("p (h n) -> p h n", h=2),
                                    EB[:, :].rearrange("p (h n) -> p h n", h=2),
                                    [[0, 2], [-1, 512]], Op.is_ge, 0.0,
                                    base=sti * 128, channel_multiplier=1))

                        if ci == 0 and w == 0 and sti < 4:
                            @block.vector
                            def _(v):
                                v.wait_ge(afd.h, afd.n)
                                ctx.sem("lbd").inc(v.tensor_scalar(
                                    EB[:, :], EB[:, :], LBM[:, 0:1], None, Op.mult))

                        @block.tensor
                        def _(t, w=w, pair=pair, sti=sti):
                            t.wait_ge(afd.h, afd.n)
                            t.wait_ge(ctx.sem("lbd").h, ctx.sem("lbd").n)
                            first = (sti == st_range.start)
                            lastst = (sti == 7)
                            ttile = w * 4 + sti
                            avd.inc(t.matmul(YPP[0][0:65, :], VT[:, ttile, 2 * pair, :],
                                             EB[:, 0:512], start=first, stop=lastst,
                                             skip_group_check=True))
                            avd.inc(t.matmul(YPP[1][0:65, :], VT[:, ttile, 2 * pair + 1, :],
                                             EB[:, 512:1024], start=first, stop=lastst,
                                             skip_group_check=True))

                    for hh in range(2):
                        @block.vector
                        def _(v, hh=hh):
                            v.wait_ge(avd.h, avd.n)
                            with nc.allow_low_precision(reason="1/Z"):
                                v.reciprocal(ZI[0:1, :], YPP[hh][64:65, :])
                            ctx.vflush(v, ctx.sem("zi"))

                        @block.tensor
                        def _(t):
                            t.wait_ge(ctx.sem("zi").h, ctx.sem("zi").n)
                            ctx.sem("zb").inc(t.matmul(SPP[:, 0:512], ONESR[:, :], ZI[0:1, :],
                                                       start=True, stop=True))

                        @block.vector
                        def _(v, w=w, pair=pair, hh=hh):
                            v.wait_ge(ctx.sem("zb").h, ctx.sem("zb").n)
                            h = 2 * pair + hh
                            ctx.chain(v, "V", [
                                lambda: v.tensor_copy(ZB[:, :], SPP[:, 0:512]),
                                lambda h=h, w=w: v.tensor_tensor(
                                    YW[(h % 2) * 64:(h % 2) * 64 + 64, h // 2, w * 512:(w + 1) * 512],
                                    YPP[hh][0:64, :], ZB[0:64, :], Op.mult),
                            ])
                            ctx.vflush(v, evd)

            XOWN = XH[:, :, OWN:OWN + 1024]
            if phase != 'ffn':
                if ci == 0:
                    ctx._wob = ctx.sb("wob", [128, 4, 128], BF16)
                emit_projF(ctx, block, f"wo{ci}", YW, woT, XOWN, 512, C, ntok=1024,
                           wait=(evd, evd.n), add_into=True, wbuf=ctx._wob,
                           psums=[PS[0], PS[1]])

            if phase == 'attn':
                @block.vector
                def _(v, ci=ci):
                    v.wait_ge(ctx.sem(f"wo{ci}ev").h, ctx.sem(f"wo{ci}ev").n)
                    ctx.vflush(v, fin)

                @block.sync
                def _(s, ci=ci):
                    s.wait_ge(fin.h, fin.n)
                    dma(s, st_, yT[:, ci * 1024:(ci + 1) * 1024].rearrange("(c p) n -> p c n", p=128),
                        XH[:, :, OWN:OWN + 1024])
                continue

            woev = ctx.sem(f"wo{ci}ev")

            def srcf(cch, q):
                off = OWN + q * 512 - (1 if cch >= 4 else 0)
                return XH[:, cch, off:off + 512]

            def outf(cch, q):
                return HLN[:, cch, q * 512:(q + 1) * 512]

            emit_ln2(ctx, block, f"li{ci}", srcf, None, LNG, LNB, ONESC, ONESR,
                     [PS[0], PS[1], PS[2], PS[3]], (xld, xld.n),
                     ctx.sem(f"lnd{ci}"), ntok=1024, SCR3=LNSCR, OUT_cols=outf)
            if DEBUG and ci == 0:
                dbgh = ctx.dout("dbgh", [C, 1024], F32)
                @block.gpsimd
                def _(g):
                    g.wait_ge(ctx.sem(f"lnd0").h, ctx.sem(f"lnd0").n)
                    dma(g, ctx.sem("dbgs"), dbgh.rearrange("(c p) n -> p c n", p=128), HLN[:, :, :])

            for sb_ in range(2):
                cb0 = OWN + sb_ * 512
                lnd = ctx.sem(f"lnd{ci}")
                dmm_gate = dmm.n
                for m in range(22):
                    mr = 128 if m < 21 else 42

                    @block.gpsimd
                    def _(g, m=m, mr=mr):
                        g.wait_ge(gmm.h, gmm.n)
                        dma(g, wqv, WF1A[:, :, 0:mr],
                            w1aT[:, m * 128:m * 128 + mr].rearrange("(c p) m -> p c m", p=128))
                        dma(g, wqv, WF1G[:, :, 0:mr],
                            w1gT[:, m * 128:m * 128 + mr].rearrange("(c p) m -> p c m", p=128))

                    @block.tensor
                    def _(t, m=m, mr=mr, sb_=sb_, ci=ci, lnd=lnd):
                        t.wait_ge(wqv.h, wqv.n)
                        if m == 0:
                            t.wait_ge(lnd.h, lnd.n)
                        t.wait_ge(gel_s.h, gel_s.n)
                        t.wait_ge(ahd.h, ahd.n)
                        for k in range(NCH):
                            last = t.matmul(PS[0][0:mr, :], WF1A[:, k, 0:mr],
                                            HLN[:, k, sb_ * 512:(sb_ + 1) * 512],
                                            start=(k == 0), stop=(k == NCH - 1))
                        gmm.inc(last)
                        for k in range(NCH):
                            last = t.matmul(PS[1][0:mr, :], WF1G[:, k, 0:mr],
                                            HLN[:, k, sb_ * 512:(sb_ + 1) * 512],
                                            start=(k == 0), stop=(k == NCH - 1))
                        gmm.inc(last)

                    @block.scalar
                    def _(sc, m=m, mr=mr):
                        sc.wait_ge(gmm.h, gmm.n)
                        gel_s.inc(sc.activation(out=SC64[0:mr, :], in_=PS[1][0:mr, :],
                                                func=AF.Gelu, bias=B1G[0:mr, m:m + 1]))

                    @block.vector
                    def _(v, m=m, mr=mr, dmm_gate=dmm_gate):
                        v.wait_ge(gel_s.h, gel_s.n)
                        v.wait_ge(dmm.h, dmm_gate)
                        ahd.inc(v.scalar_tensor_tensor(AH[0:mr, m, :], PS[0][0:mr, :],
                                                       B1A[0:mr, m:m + 1], SC64[0:mr, :],
                                                       Op.add, Op.mult))

                for oc in range(NCH):
                    @block.gpsimd
                    def _(g, oc=oc):
                        g.wait_ge(dmm.h, dmm.n)
                        for k in range(22):
                            kr = 128 if k < 21 else 42
                            dma(g, w2s, WB2[0:kr, k, :],
                                w2T[k * 128:k * 128 + kr, oc * 128:(oc + 1) * 128])

                    @block.tensor
                    def _(t, oc=oc, sb_=sb_):
                        t.wait_ge(w2s.h, w2s.n)
                        if oc == 0:
                            t.wait_ge(ahd.h, ahd.n)
                        t.wait_ge(xup.h, xup.n)
                        for k in range(22):
                            kr = 128 if k < 21 else 42
                            last = t.matmul(PS[2][:, :], WB2[0:kr, k, :],
                                            AH[0:kr, k, :], start=(k == 0), stop=(k == 21))
                        dmm.inc(last)

                    @block.vector
                    def _(v, oc=oc, sb_=sb_, cb0=cb0):
                        v.wait_ge(dmm.h, dmm.n)
                        v.scalar_tensor_tensor(XH[:, oc, cb0:cb0 + 512], PS[2][:, :],
                                               B2C[:, oc:oc + 1], XH[:, oc, cb0:cb0 + 512],
                                               Op.add, Op.add)
                        ctx.vflush(v, xup)

            if wrap:
                for sbw in range(2):
                    ob = OWN + sbw * 512
                    for cch in range(NCH):
                        buf = SC64 if cch % 2 == 0 else RH
                        x0s = ctx.sem(f"x0l{cch % 2}")

                        @block.sync
                        def _(s, ci=ci, sbw=sbw, cch=cch, buf=buf, x0s=x0s):
                            s.wait_ge(xup.h, xup.n)
                            s.wait_ge(ctx.sem("wadd").h, ctx.sem("wadd").n)
                            dma(s, x0s, buf[:, :],
                                xin0[cch * 128:(cch + 1) * 128,
                                     ci * 1024 + sbw * 512:ci * 1024 + (sbw + 1) * 512])

                        @block.vector
                        def _(v, cch=cch, ob=ob, buf=buf, x0s=x0s, sbw=sbw):
                            v.wait_ge(x0s.h, x0s.n)
                            v.tensor_tensor(XH[:, cch, ob:ob + 512], XH[:, cch, ob:ob + 512],
                                            buf[:, :], Op.add)
                            ctx.vflush(v, ctx.sem("wadd"))

                    def src2(cch, q, ob=ob):
                        return XH[:, cch, ob:ob + 512]

                    emit_ln2(ctx, block, f"wl{ci}{sbw}", src2, None, None, None,
                             ONESC, ONESR, [PS[0], PS[1], PS[2], PS[3]],
                             (ctx.sem("wadd"), ctx.sem("wadd").n), ctx.sem("wlnd"),
                             ntok=512, SCR3=EB.rearrange("p (a b) -> p a b", a=2),
                             OUT_cols=src2)

                @block.vector
                def _(v):
                    v.wait_ge(ctx.sem("wlnd").h, ctx.sem("wlnd").n)
                    ctx.vflush(v, fin)

                @block.sync
                def _(s, ci=ci):
                    s.wait_ge(fin.h, fin.n)
                    dma(s, st_, yT[:, ci * 1024:(ci + 1) * 1024].rearrange("(c p) n -> p c n", p=128),
                        XH[:, :, OWN:OWN + 1024])
            else:
                @block.vector
                def _(v, ci=ci):
                    v.wait_ge(xup.h, xup.n)
                    ctx.vflush(v, fin)

                @block.sync
                def _(s, ci=ci):
                    s.wait_ge(fin.h, fin.n)
                    dma(s, st_, yT[:, ci * 1024:(ci + 1) * 1024].rearrange("(c p) n -> p c n", p=128),
                        XH[:, :, OWN:OWN + 1024])

        @block.sync
        def _(s):
            s.wait_ge(st_.h, st_.n)

    return ctx


def _rotary_tables():
    DH, WINDOW = 64, 512
    n = T
    half = np.arange(0, DH, 2, dtype=np.float32)
    inv = 1.0 / 10000.0 ** (half / DH)
    t = np.arange(n, dtype=np.float32)
    fr = t[:, None] * inv[None, :]
    fr = np.concatenate([fr, fr], -1)
    base = (half + 0.4 * DH) / (1.4 * DH)
    power = (t - n // 2) / float(WINDOW // 2)
    sc = base[None, :] ** power[:, None]
    sc = np.concatenate([sc, sc], -1)
    cos, sin = np.cos(fr), np.sin(fr)
    csq = (cos * sc).T
    snq = (sin * sc).T
    csk = (cos / sc).T
    snk = (sin / sc).T
    out = []
    for tab in (csq, snq, csk, snk):
        out.append(np.ascontiguousarray(np.tile(tab, (2, 1)).astype(np.float32)))
    return out


def launch_lffn(x, W, pfx):
    REG, OWN = 1538, 514
    csq, snq, csk, snk = _rotary_tables()
    x_in0 = x
    for l in range(2):
        wqkvT = np.ascontiguousarray(W['wqkv'][l])
        woT = np.ascontiguousarray(W['wo'][l])
        w1aT = np.ascontiguousarray(W['w1'][l][:, 0:2730])
        w1gT = np.ascontiguousarray(W['w1'][l][:, 2730:5460])
        w2T = np.ascontiguousarray(W['w2'][l])
        qsr = np.concatenate([W['q_scale'][l], W['k_scale'][l]]).reshape(1, 128)
        qs2 = np.stack([np.tile(W['q_scale'][l], 2), np.tile(W['k_scale'][l], 2)], 1)
        b1a = np.zeros((128, 22), np.float32)
        b1g = np.zeros((128, 22), np.float32)
        for m in range(22):
            mr = 128 if m < 21 else 42
            b1a[0:mr, m] = W['b1'][l][m * 128:m * 128 + mr]
            b1g[0:mr, m] = W['b1'][l][2730 + m * 128:2730 + m * 128 + mr]
        b2c = np.ascontiguousarray(W['b2'][l].reshape(NCH, 128).T)
        lng = np.ascontiguousarray(W['ln_g'][l].reshape(NCH, 128).T)
        lnb = np.ascontiguousarray(W['ln_b'][l].reshape(NCH, 128).T)
        hselt = np.zeros((2, 128), np.float32); hselt[0, 0:64] = 1; hselt[1, 64:128] = 1
        hsel = np.ascontiguousarray(hselt.T)
        wrap = (l == 1)
        for phase in ('attn', 'ffn'):
            out = np.empty_like(x)
            ctx = build_lffn_inner(0, None, wrap and phase == 'ffn', phase=phase)
            maps = []
            for c in range(NCORES):
                bb, sl = core_slices(c, True)
                xh = np.zeros((C, 2 * REG), np.float32)
                for ci, (s0, s1) in enumerate(sl):
                    lo = s0 - OWN
                    pad = max(0, -lo)
                    xh[:, ci * REG + pad:(ci + 1) * REG] = x[bb, lo + pad:s1, :].T
                lbm = np.full((128, 1), 0.0 if sl[0][0] == 0 else 1.0, np.float32)
                m = {"xh": xh, "wqkvT": wqkvT, "woT": woT, "w1aT": w1aT, "w1gT": w1gT,
                     "w2T": w2T, "qsr": qsr, "qs2": qs2, "b1a_c": b1a, "b1g_c": b1g,
                     "b2_c": b2c, "lng_c": lng, "lnb_c": lnb,
                     "csq": _reg2(csq, sl, REG, OWN), "snq": _reg2(snq, sl, REG, OWN),
                     "csk": _reg2(csk, sl, REG, OWN), "snk": _reg2(snk, sl, REG, OWN),
                     "c_onesc": np.ones((128, 1), np.float32),
                     "c_onesr": np.ones((1, 128), np.float32),
                     "c_hselt": hselt, "c_hsel": hsel, "lbm": lbm,
                     "c_vones": np.ones((128, 96), np.float32)}
                if wrap and phase == 'ffn':
                    xin = np.concatenate([x_in0[bb, s0:s1, :] for (s0, s1) in sl], axis=0)
                    m["xin0"] = np.ascontiguousarray(xin.T)
                maps.append(m)
            res = run_bass_kernel_spmd(ctx.nc, maps, core_ids=list(range(NCORES)))
            for c in range(NCORES):
                bb, sl = core_slices(c, True)
                o = 0
                for (s0, s1) in sl:
                    n = s1 - s0
                    out[bb, s0:s1, :] = res.results[c]["yT"][:, o:o + n].T
                    o += n
            x = out
    return x


def _reg2(tab, sl, REG, OWN):
    out = np.zeros((128, 2 * REG), np.float32)
    for ci, (s0, s1) in enumerate(sl):
        lo = s0 - OWN
        pad = max(0, -lo)
        out[:, ci * REG + pad:(ci + 1) * REG] = tab[:, lo + pad:lo + REG]
    return out


BF16 = mybir.dt.bfloat16
L2_HALO = 4
L2_NTK = 1024 + L2_HALO


def build_l2a():
    ctx = Ctx("l2a")
    nc = ctx.nc
    NTK = L2_NTK
    xh = ctx.din("xh", [C, 2 * NTK])
    inwT = ctx.din("inwT", [C, 4384])
    nw_c = ctx.din("nw_c", [128, NCH], F32)
    wcv_c = ctx.din("wcv_c", [128, 64], F32)
    cvb_c = ctx.din("cvb_c", [128, 16], F32)
    dcol_c = ctx.din("dcol_c", [128, 16], F32)
    alog_c = ctx.din("alog_c", [32, 1], F32)
    dtb_c = ctx.din("dtb_c", [32, 1], F32)
    c_one32 = ctx.din("c_one32", [32, 1], F32)
    ones32 = ctx.din("ones32", [32, 1024], F32)
    c_onesc = ctx.din("c_onesc", [128, 1])
    c_onesr = ctx.din("c_onesr", [1, 128])
    c_hselt = ctx.din("c_hselt", [2, 128], F32)
    c_onesrf = ctx.din("c_onesrf", [1, 128], F32)
    c_ident = ctx.din("c_ident", [128, 128], F32)
    c_identr = ctx.din("c_identr", [128, 128])

    p1T = ctx.dout("p1T", [2048, NT])
    szT = ctx.dout("szT", [2048, NT])
    cmTo = ctx.dout("cmT", [128, NT])
    eagr = ctx.dout("eagr", [32, NT], F32)
    schk = ctx.dout("schk", [256, 2048])

    agrel_d = nc.dram_tensor("agrel_d", [32, 1024], F32, kind="Internal").ap()
    erel_d = nc.dram_tensor("erel_d", [32, 1024], F32, kind="Internal").ap()
    decrow_d = nc.dram_tensor("decrow_d", [1, 256], F32, kind="Internal").ap()

    XT = ctx.sb("XT", [128, NCH, NTK])
    XS = ctx.sb("XS", [128, 16, NTK], BF16)
    XC = ctx.sb("XC", [128, 16, 1024], BF16)
    BMT = ctx.sb("BMT", [128, 1024])
    CMT = ctx.sb("CMT", [128, 1024])
    SST = ctx.sb("SST", [128, 2048])
    WPRJ = ctx.sb("WPRJ", [128, 2, NCH, 128])
    SQ = ctx.sb("SQ", [128, 2, 512])
    ZEV = ctx.sb("ZEV", [128, 2, 512])
    CVS = ctx.sb("CVS", [128, 512])
    RW = ctx.sb("RW", [1, 4, 512], F32)
    DTR = ctx.sb("DTR", [32, 1024], F32)
    AGR = ctx.sb("AGR", [32, 1024], F32)
    AGL = ctx.sb("AGL", [32, 1024], F32)
    ERL = ctx.sb("ERL", [32, 1024], F32)
    WRR = ctx.sb("WRR", [32, 1024], F32)
    AEXP = ctx.sb("AEXP", [32, 1], F32)
    NEGC = ctx.sb("NEGC", [32, 1], F32)
    AGT = ctx.sb("AGT", [128, 8, 32], F32)
    DTT = ctx.sb("DTT", [128, 8, 32], F32)
    WTT = ctx.sb("WTT", [128, 8, 32], F32)
    DECB = ctx.sb("DECB", [128, 256], F32)
    DROW = ctx.sb("DROW", [1, 256], F32)
    EAGS = ctx.sb("EAGS", [32, 1024], F32)
    AGP2e = ctx.sb("AGP2e", [1, 1024], F32)
    AGP2o = ctx.sb("AGP2o", [1, 1024], F32)
    ERP2 = ctx.sb("ERP2", [2, 1024], F32)
    G8 = ctx.sb("G8", [128, 8, 128])
    BT8 = ctx.sb("BT8", [128, 8, 128])
    XCDT8 = ctx.sb("XCDT8", [128, 8, 128])
    XCW8 = ctx.sb("XCW8", [128, 8, 128])
    MT = ctx.sb("MT", [128, 128])
    EDG = ctx.sb("EDG", [128, 128])
    PEV = ctx.sb("PEV", [128, 2, 128])
    EGS = ctx.sb("EGS", [128, 128])
    IDNT = ctx.sb("IDNT", [128, 128], F32)
    IDNB = ctx.sb("IDNB", [128, 128], BF16)
    ONESC = ctx.sb("ONESC", [128, 1])
    ONESR = ctx.sb("ONESR", [1, 128])
    ONESRF = ctx.sb("ONESRF", [1, 128], F32)
    HSELT = ctx.sb("HSELT", [2, 128], F32)
    IDNR = ctx.sb("IDNR", [128, 128])
    NWC = ctx.sb("NWC", [128, NCH], F32)
    WCV = ctx.sb("WCV", [128, 64], F32)
    CVB = ctx.sb("CVB", [128, 16], F32)
    DCOL = ctx.sb("DCOL", [128, 16], F32)
    ALOG = ctx.sb("ALOG", [32, 1], F32)
    DTB = ctx.sb("DTB", [32, 1], F32)
    ONE32 = ctx.sb("ONE32", [32, 1], F32)
    ON32 = ctx.sb("ON32", [32, 1024], F32)

    PB = [ctx.ps(f"pb{i}", [128, 512]) for i in range(8)]

    ld = ctx.sem("ld")
    xld = ctx.sem("xld")
    wp = ctx.sem("wp")
    sq_s = ctx.sem("sq")
    smm = ctx.sem("smm")
    srw = ctx.sem("srw")
    sbc = ctx.sem("sbc")
    xnd = ctx.sem("xnd")
    pmm = ctx.sem("pmm")
    pev = ctx.sem("pev")
    zst = ctx.sem("zst")
    cst = ctx.sem("cst")
    cvv = ctx.sem("cvv")
    cvs_s = ctx.sem("cvs")
    rwd = ctx.sem("rwd")
    rsc = ctx.sem("rsc")
    tpd = ctx.sem("tpd")
    ted = ctx.sem("ted")
    bnc = ctx.sem("bnc")
    gte = ctx.sem("gte")
    agl_l = ctx.sem("agl")
    ebc = ctx.sem("ebc")
    edd = ctx.sem("edd")
    mtv = ctx.sem("mtv")
    afd = ctx.sem("afd")
    avm = ctx.sem("avm")
    sel = ctx.sem("sel")
    pvd = ctx.sem("pvd")
    pst = ctx.sem("pst")
    sud = ctx.sem("sud")
    fin = ctx.sem("fin")

    qs_own = [(4, 512), (516, 512)]
    qs_halo = [(0, 512), (512, 512), (1024, 4)]

    with nc.Block() as block:
        @block.sync
        def _(s):
            dma(s, ld, ONESC[:, :], c_onesc[:, :])
            dma(s, ld, ONESR[:, :], c_onesr[:, :])
            dma(s, ld, HSELT[:, :], c_hselt[:, :])
            dma(s, ld, ONESRF[:, :], c_onesrf[:, :])
            dma(s, ld, IDNR[:, :], c_identr[:, :])
            dma(s, ld, IDNT[:, :], c_ident[:, :])
            dma(s, ld, NWC[:, :], nw_c[:, :])
            dma(s, ld, WCV[:, :], wcv_c[:, :])
            dma(s, ld, CVB[:, :], cvb_c[:, :])
            dma(s, ld, DCOL[:, :], dcol_c[:, :])
            dma(s, ld, ALOG[:, :], alog_c[:, :])
            dma(s, ld, DTB[:, :], dtb_c[:, :])
            dma(s, ld, ONE32[:, :], c_one32[:, :])
            dma(s, ld, ON32[:, :], ones32[:, :])

        @block.gpsimd
        def _(g):
            dma(g, ld, IDNB[:, :], c_ident[:, :])

        @block.scalar
        def _(sc):
            sc.wait_ge(ld.h, ld.n)
            rsc.inc(sc.activation(out=AEXP[:, :], in_=ALOG[:, :], func=AF.Exp))

        for ci in range(2):
            @block.sync
            def _(s, ci=ci):
                if ci == 1:
                    s.wait_ge(fin.h, fin.n)
                dma(s, xld, XT[:, :, :],
                    xh[:, ci * NTK:(ci + 1) * NTK].rearrange("(c p) n -> p c n", p=128))

            for (qc, qn) in qs_halo:
                @block.scalar
                def _(sc, qc=qc, qn=qn):
                    sq_base = smm.n
                    for cch in range(NCH):
                        if cch == 0:
                            sc.wait_ge(xld.h, xld.n)
                            sc.wait_ge(xnd.h, xnd.n)
                        if cch >= 2:
                            sc.wait_ge(smm.h, sq_base + cch - 1)
                        sq_s.inc(sc.activation(out=SQ[:, cch % 2, 0:qn],
                                               in_=XT[:, cch, qc:qc + qn], func=AF.Square))

                @block.tensor
                def _(t, qc=qc, qn=qn):
                    t.wait_ge(srw.h, srw.n)
                    for cch in range(NCH):
                        t.wait_ge(sq_s.h, sq_s.n - (NCH - 1) + cch)
                        smm.inc(t.matmul(PB[6][0:1, 0:qn], ONESC, SQ[:, cch % 2, 0:qn],
                                         start=(cch == 0), stop=(cch == NCH - 1)))

                @block.vector
                def _(v, qn=qn):
                    v.wait_ge(smm.h, smm.n)
                    v.tensor_scalar(RW[0:1, 1, 0:qn], PB[6][0:1, 0:qn],
                                    1.0 / C, 1e-6, Op.mult, Op.add)
                    ctx.vflush(v, srw)

                @block.scalar
                def _(sc, qn=qn):
                    sc.wait_ge(srw.h, srw.n)
                    sc.activation(out=RW[0:1, 2, 0:qn], in_=RW[0:1, 1, 0:qn], func=AF.Sqrt)
                    ctx.aflush(sc, rsc)

                @block.vector
                def _(v, qn=qn):
                    v.wait_ge(rsc.h, rsc.n)
                    with nc.allow_low_precision(reason="rstd"):
                        v.reciprocal(RW[0:1, 1, 0:qn], RW[0:1, 2, 0:qn])
                    ctx.vflush(v, srw)

                @block.tensor
                def _(t, qn=qn):
                    t.wait_ge(srw.h, srw.n)
                    sbc.inc(t.matmul(PB[7][:, 0:qn], ONESRF, RW[0:1, 1, 0:qn],
                                     start=True, stop=True))

                @block.vector
                def _(v, qc=qc, qn=qn):
                    v.wait_ge(sbc.h, sbc.n)
                    for cch in range(NCH):
                        ctx.chain(v, "V", [
                            lambda cch=cch: v.tensor_tensor(
                                XT[:, cch, qc:qc + qn], XT[:, cch, qc:qc + qn],
                                PB[7][:, 0:qn], Op.mult),
                            lambda cch=cch: v.tensor_scalar(
                                XT[:, cch, qc:qc + qn], XT[:, cch, qc:qc + qn],
                                NWC[:, cch:cch + 1], None, Op.mult),
                        ])
                    ctx.vflush(v, xnd)

            qglob = [0]
            for mc in range(35):
                mr = 32 if mc == 34 else 128
                kind = ('z' if mc < 16 else 'xs' if mc < 32 else
                        'B' if mc == 32 else 'C' if mc == 33 else 'dt')
                qp = qs_halo if kind == 'xs' else qs_own
                buf = mc % 2

                @block.sync
                def _(s, mc=mc, mr=mr, buf=buf):
                    s.wait_ge(pmm.h, max(0, pmm.n - len(qp) + 0 if False else pmm.n))
                    dma(s, wp, WPRJ[:, buf, :, 0:mr],
                        inwT[:, mc * 128:mc * 128 + mr].rearrange("(c p) m -> p c m", p=128))

                for (qc, qn) in qp:
                    pb = PB[qglob[0] % 2]
                    qglob[0] += 1

                    @block.tensor
                    def _(t, mc=mc, mr=mr, qc=qc, qn=qn, pb=pb, buf=buf):
                        t.wait_ge(wp.h, wp.n)
                        t.wait_ge(xnd.h, xnd.n)
                        t.wait_ge(pev.h, max(0, pev.n - 1))
                        for k in range(NCH):
                            last = t.matmul(pb[0:mr, 0:qn], WPRJ[:, buf, k, 0:mr],
                                            XT[:, k, qc:qc + qn],
                                            start=(k == 0), stop=(k == NCH - 1))
                        pmm.inc(last)

                    if kind == 'z':
                        @block.scalar
                        def _(sc, mc=mc, qc=qc, qn=qn, pb=pb):
                            sc.wait_ge(pmm.h, pmm.n)
                            sc.wait_ge(zst.h, max(0, zst.n - 16))
                            sc.activation(out=ZEV[:, qglob[0] % 2, 0:qn], in_=pb[0:128, 0:qn],
                                          func=AF.Silu)
                            ctx.aflush(sc, pev)

                        @block.gpsimd
                        def _(g, mc=mc, qc=qc, qn=qn, ci=ci):
                            g.wait_ge(pev.h, pev.n)
                            dma(g, zst, szT[mc * 128:(mc + 1) * 128,
                                            ci * 1024 + qc - 4:ci * 1024 + qc - 4 + qn],
                                ZEV[:, qglob[0] % 2, 0:qn])
                    elif kind == 'xs':
                        @block.vector
                        def _(v, mc=mc, qc=qc, qn=qn, pb=pb):
                            v.wait_ge(pmm.h, pmm.n)
                            pev.inc(v.tensor_copy(XS[:, mc - 16, qc:qc + qn], pb[0:128, 0:qn]))
                    elif kind in ('B', 'C'):
                        DST = BMT if kind == 'B' else CMT

                        @block.vector
                        def _(v, qc=qc, qn=qn, pb=pb, DST=DST):
                            v.wait_ge(pmm.h, pmm.n)
                            pev.inc(v.tensor_copy(DST[:, qc - 4:qc - 4 + qn], pb[0:128, 0:qn]))

                        if kind == 'C':
                            @block.gpsimd
                            def _(g, qc=qc, qn=qn, ci=ci):
                                g.wait_ge(pev.h, pev.n)
                                dma(g, cst, cmTo[:, ci * 1024 + qc - 4:ci * 1024 + qc - 4 + qn],
                                    CMT[:, qc - 4:qc - 4 + qn])
                    else:
                        dts = ctx.sem("dts")

                        @block.vector
                        def _(v, qc=qc, qn=qn, pb=pb):
                            v.wait_ge(pmm.h, pmm.n)
                            v.wait_ge(cst.h, cst.n)
                            v.wait_ge(bnc.h, bnc.n)
                            v.wait_ge(tpd.h, tpd.n)
                            v.tensor_scalar(EAGS[:, 0:qn], pb[0:32, 0:qn], DTB[:, 0:1],
                                            None, Op.add)
                            ctx.vflush(v, dts)

                        @block.scalar
                        def _(sc, qn=qn):
                            sc.wait_ge(dts.h, dts.n)
                            ctx.chain(sc, "SC", [
                                lambda: sc.activation(out=AGL[:, 0:qn], in_=EAGS[:, 0:qn],
                                                      func=AF.Abs),
                                lambda: sc.activation(out=WRR[:, 0:qn], in_=AGL[:, 0:qn],
                                                      func=AF.Exp, scale=-1.0),
                                lambda: sc.activation(out=AGL[:, 0:qn], in_=WRR[:, 0:qn],
                                                      func=AF.Ln, bias=ONE32[:, 0:1]),
                            ])
                            ctx.aflush(sc, dts)

                        @block.vector
                        def _(v, qc=qc, qn=qn):
                            v.wait_ge(dts.h, dts.n)
                            ctx.chain(v, "V", [
                                lambda: v.tensor_scalar(EAGS[:, 0:qn], EAGS[:, 0:qn], 0.0,
                                                        None, Op.max),
                                lambda: v.tensor_tensor(DTR[:, qc - 4:qc - 4 + qn],
                                                        EAGS[:, 0:qn], AGL[:, 0:qn], Op.add),
                            ])
                            ctx.vflush(v, pev)

            for cch in range(16):
                for q2 in range(2):
                    @block.vector
                    def _(v, cch=cch, q2=q2):
                        if cch == 0 and q2 == 0:
                            v.wait_ge(pev.h, pev.n)
                        v.wait_ge(cvs_s.h, cvs_s.n)
                        ops = [lambda: v.tensor_scalar(
                            CVS[:, :], XS[:, cch, 1 + q2 * 512:1 + q2 * 512 + 512],
                            WCV[:, cch * 4:cch * 4 + 1], None, Op.mult)]
                        for k_ in range(1, 4):
                            ops.append(lambda k_=k_: v.scalar_tensor_tensor(
                                CVS[:, :], XS[:, cch, 1 + k_ + q2 * 512:1 + k_ + q2 * 512 + 512],
                                WCV[:, cch * 4 + k_:cch * 4 + k_ + 1], CVS[:, :],
                                Op.mult, Op.add))
                        ctx.chain(v, "V", ops)
                        ctx.vflush(v, cvv)

                    @block.scalar
                    def _(sc, cch=cch, q2=q2):
                        sc.wait_ge(cvv.h, cvv.n)
                        sc.activation(out=XC[:, cch, q2 * 512:(q2 + 1) * 512], in_=CVS[:, :],
                                      func=AF.Silu, bias=CVB[:, cch:cch + 1])
                        ctx.aflush(sc, cvs_s)

            @block.vector
            def _(v, ci=ci):
                v.wait_ge(pev.h, pev.n)
                v.wait_ge(rsc.h, rsc.n)
                v.wait_ge(tpd.h, tpd.n)
                v.wait_ge(bnc.h, bnc.n)
                ctx.chain(v, "V", [
                    lambda: v.tensor_scalar(ERL[:, :], DTR[:, :], AEXP[:, 0:1], None, Op.mult),
                    lambda: v.tensor_tensor_scan(AGR[:, :], ON32[:, :], ERL[:, :],
                                                 0.0, Op.mult, Op.add),
                ])
                ops = []
                for si in range(8):
                    if si == 0:
                        ops.append(lambda: v.tensor_copy(AGL[:, 0:128], AGR[:, 0:128]))
                    else:
                        ops.append(lambda si=si: v.tensor_scalar(
                            AGL[:, si * 128:(si + 1) * 128], AGR[:, si * 128:(si + 1) * 128],
                            AGR[:, si * 128 - 1:si * 128], None, Op.subtract))
                ctx.chain(v, "V", ops)
                ctx.vflush(v, rwd)

            @block.scalar
            def _(sc):
                sc.wait_ge(rwd.h, rwd.n)
                sc.activation(out=ERL[:, :], in_=AGL[:, :], func=AF.Exp, scale=-1.0)
                ctx.aflush(sc, rsc)

            for si in range(8):
                @block.vector
                def _(v, si=si):
                    v.wait_ge(rwd.h, rwd.n)
                    v.wait_ge(rsc.h, rsc.n)
                    v.tensor_scalar(NEGC[:, :], AGL[:, si * 128 + 127:si * 128 + 128],
                                    -1.0, None, Op.mult)
                    ctx.vflush(v, rwd)

                @block.scalar
                def _(sc, si=si):
                    sc.wait_ge(rwd.h, rwd.n)
                    sc.activation(out=WRR[:, si * 128:(si + 1) * 128],
                                  in_=AGL[:, si * 128:(si + 1) * 128],
                                  func=AF.Exp, bias=NEGC[:, 0:1])
                    ctx.aflush(sc, rsc)

            @block.vector
            def _(v, ci=ci):
                v.wait_ge(rsc.h, rsc.n)
                ctx.chain(v, "V", [
                    lambda: v.tensor_tensor(WRR[:, :], WRR[:, :], DTR[:, :], Op.mult),
                ])
                ctx.vflush(v, rwd)

            @block.scalar
            def _(sc, ci=ci):
                sc.wait_ge(rwd.h, rwd.n)
                sc.activation(out=EAGS[:, :], in_=AGR[:, :], func=AF.Exp, scale=-1.0)
                ctx.aflush(sc, rsc)

            @block.gpsimd
            def _(g, ci=ci):
                g.wait_ge(rsc.h, rsc.n)
                dma(g, cst, eagr[:, ci * 1024:(ci + 1) * 1024], EAGS[:, :])
                dma(g, bnc, agrel_d[:, :], AGL[:, :])
                dma(g, bnc, erel_d[:, :], ERL[:, :])
                for si in range(8):
                    dma(g, bnc, decrow_d[0:1, si * 32:(si + 1) * 32],
                        ERL[:, si * 128 + 127:si * 128 + 128])

            @block.sync
            def _(s):
                s.wait_ge(bnc.h, bnc.n)
                s.wait_ge(sbc.h, sbc.n)
                dma(s, agl_l, DROW[:, :], decrow_d[:, :])

            @block.tensor
            def _(t):
                t.wait_ge(agl_l.h, agl_l.n)
                t.wait_ge(sud.h, sud.n)
                sbc.inc(t.matmul(PB[6][:, 0:256], ONESRF, DROW[:, :], start=True, stop=True))

            @block.vector
            def _(v):
                v.wait_ge(sbc.h, sbc.n)
                pvd.inc(v.tensor_copy(DECB[:, :], PB[6][:, 0:256]))

            for (SRC, DSTC) in ((AGL, AGT), (DTR, DTT), (WRR, WTT)):
                for si in range(8):
                    @block.tensor
                    def _(t, SRC=SRC, si=si):
                        t.wait_ge(rwd.h, rwd.n)
                        t.wait_ge(rsc.h, rsc.n)
                        t.wait_ge(ted.h, ted.n)
                        tpd.inc(t.matmul(PB[5][0:128, 0:32],
                                         SRC[:, si * 128:(si + 1) * 128],
                                         IDNT[0:32, 0:32], start=True, stop=True))

                    @block.vector
                    def _(v, DSTC=DSTC, si=si):
                        v.wait_ge(tpd.h, tpd.n)
                        ted.inc(v.tensor_copy(DSTC[:, si, :], PB[5][0:128, 0:32]))

            for si in range(8):
                @block.tensor
                def _(t, si=si):
                    t.wait_ge(pev.h, pev.n)
                    t.wait_ge(gte.h, gte.n)
                    tpd.inc(t.matmul(PB[5][0:128, 0:128],
                                     BMT[:, si * 128:(si + 1) * 128], IDNR,
                                     start=True, stop=True))

                @block.vector
                def _(v, si=si):
                    v.wait_ge(tpd.h, tpd.n)
                    gte.inc(v.tensor_copy(BT8[:, si, :], PB[5][0:128, 0:128]))

                @block.tensor
                def _(t, si=si):
                    t.wait_ge(gte.h, gte.n)
                    tpd.inc(t.matmul(PB[5][0:128, 0:128], BMT[:, si * 128:(si + 1) * 128],
                                     CMT[:, si * 128:(si + 1) * 128], start=True, stop=True))

                @block.vector
                def _(v, si=si):
                    v.wait_ge(tpd.h, tpd.n)
                    gte.inc(v.tensor_copy(G8[:, si, :], PB[5][0:128, 0:128]))

            for cch in range(16):
                @block.sync
                def _(s, cch=cch, ci=ci):
                    s.wait_ge(bnc.h, bnc.n)
                    s.wait_ge(ebc.h, ebc.n)
                    s.wait_ge(sel.h, sel.n)
                    s.wait_ge(pvd.h, pvd.n)
                    s.wait_ge(sud.h, sud.n)
                    s.wait_ge(pst.h, pst.n)
                    dma(s, agl_l, AGP2e[:, :], agrel_d[2 * cch:2 * cch + 1, :])
                    dma(s, agl_l, AGP2o[:, :], agrel_d[2 * cch + 1:2 * cch + 2, :])
                    dma(s, agl_l, ERP2[:, :], erel_d[2 * cch:2 * cch + 2, :])

                for si in range(8):
                    @block.tensor
                    def _(t, cch=cch, si=si):
                        t.wait_ge(ld.h, ld.n)
                        t.wait_ge(cvs_s.h, cvs_s.n)
                        t.wait_ge(ted.h, ted.n)
                        t.wait_ge(gte.h, gte.n)
                        t.wait_ge(agl_l.h, agl_l.n)
                        tpd.inc(t.matmul(PB[5][0:128, 0:128],
                                         XC[:, cch, si * 128:(si + 1) * 128], IDNB,
                                         start=True, stop=True))

                    @block.vector
                    def _(v, cch=cch, si=si):
                        v.wait_ge(tpd.h, tpd.n)
                        v.wait_ge(avm.h, avm.n)
                        for hh in range(2):
                            h = 2 * cch + hh
                            ted.inc(v.tensor_scalar(
                                XCDT8[:, si, hh * 64:(hh + 1) * 64],
                                PB[5][0:128, hh * 64:(hh + 1) * 64],
                                DTT[:, si, h:h + 1], None, Op.mult))
                            ted.inc(v.tensor_scalar(
                                XCW8[:, si, hh * 64:(hh + 1) * 64],
                                PB[5][0:128, hh * 64:(hh + 1) * 64],
                                WTT[:, si, h:h + 1], None, Op.mult))

                for si in range(8):
                    for hh in range(2):
                        h = 2 * cch + hh
                        AGP = AGP2e if hh == 0 else AGP2o

                        @block.tensor
                        def _(t, si=si, hh=hh, AGP=AGP):
                            t.wait_ge(agl_l.h, agl_l.n)
                            t.wait_ge(edd.h, edd.n)
                            t.wait_ge(mtv.h, mtv.n)
                            t.wait_ge(pvd.h, pvd.n)
                            t.wait_ge(sud.h, sud.n)
                            ebc.inc(t.matmul(PB[7][:, hh * 128:(hh + 1) * 128], ONESRF,
                                             AGP[0:1, si * 128:(si + 1) * 128],
                                             start=True, stop=True))

                        @block.scalar
                        def _(sc, si=si, hh=hh, h=h):
                            sc.wait_ge(ebc.h, ebc.n)
                            sc.wait_ge(ted.h, ted.n)
                            sc.wait_ge(mtv.h, mtv.n)
                            sc.activation(out=EDG[:, :], in_=PB[7][:, hh * 128:(hh + 1) * 128],
                                          func=AF.Exp, scale=-1.0, bias=AGT[:, si, h:h + 1])
                            ctx.aflush(sc, edd)

                        @block.vector
                        def _(v, si=si, h=h):
                            v.wait_ge(edd.h, edd.n)
                            v.wait_ge(gte.h, gte.n)
                            v.wait_ge(afd.h, afd.n)
                            v.wait_ge(avm.h, avm.n)
                            v.tensor_tensor(MT[:, :], G8[:, si, :], EDG[:, :], Op.mult)
                            ctx.vflush(v, mtv)

                        @block.gpsimd
                        def _(g, si=si):
                            g.wait_ge(mtv.h, mtv.n)
                            afd.inc(g.affine_select(MT[:, :], MT[:, :], [[1, 128]],
                                                    Op.is_ge, 0.0, base=0,
                                                    channel_multiplier=-1))

                        @block.tensor
                        def _(t, si=si, hh=hh, h=h, cch=cch):
                            t.wait_ge(afd.h, afd.n)
                            t.wait_ge(ted.h, ted.n)
                            t.wait_ge(pvd.h, pvd.n)
                            t.wait_ge(sud.h, sud.n)
                            avm.inc(t.matmul(PB[2][0:64, hh * 128:(hh + 1) * 128],
                                             XCDT8[:, si, hh * 64:(hh + 1) * 64], MT[:, :],
                                             start=True, stop=True))
                            if si > 0:
                                avm.inc(t.matmul(PB[3][0:64, hh * 128:(hh + 1) * 128],
                                                 SST[:, h * 64:(h + 1) * 64],
                                                 CMT[:, si * 128:(si + 1) * 128],
                                                 start=True, stop=True))
                            avm.inc(t.matmul(PB[4][:, hh * 64:(hh + 1) * 64],
                                             BT8[:, si, :], XCW8[:, si, hh * 64:(hh + 1) * 64],
                                             start=True, stop=True))

                        @block.vector
                        def _(v, si=si, hh=hh, h=h, ci=ci):
                            v.wait_ge(avm.h, avm.n)
                            if si == 0:
                                sud.inc(v.tensor_copy(SST[:, h * 64:(h + 1) * 64],
                                                      PB[4][:, hh * 64:(hh + 1) * 64]))
                            else:
                                sud.inc(v.scalar_tensor_tensor(
                                    SST[:, h * 64:(h + 1) * 64], SST[:, h * 64:(h + 1) * 64],
                                    DECB[:, si * 32 + h:si * 32 + h + 1],
                                    PB[4][:, hh * 64:(hh + 1) * 64], Op.mult, Op.add))

                    @block.tensor
                    def _(t, cch=cch, si=si):
                        t.wait_ge(agl_l.h, agl_l.n)
                        t.wait_ge(pvd.h, pvd.n)
                        sel.inc(t.matmul(PB[6][:, 256 + si % 2 * 128:256 + si % 2 * 128 + 128]
                                         if False else PB[6][:, 0:128],
                                         HSELT, ERP2[0:2, si * 128:(si + 1) * 128],
                                         start=True, stop=True))

                    @block.vector
                    def _(v, cch=cch, si=si, ci=ci):
                        v.wait_ge(sel.h, sel.n)
                        v.wait_ge(avm.h, avm.n)
                        v.wait_ge(pst.h, pst.n)
                        pe = PEV[:, si % 2, :]
                        if si == 0:
                            ctx.chain(v, "V", [
                                lambda: v.tensor_copy(pe[0:64, :], PB[2][0:64, 0:128]),
                                lambda: v.tensor_copy(pe[64:128, :], PB[2][0:64, 128:256]),
                                lambda: v.scalar_tensor_tensor(
                                    pe, XC[:, cch, si * 128:(si + 1) * 128],
                                    DCOL[:, cch:cch + 1], pe, Op.mult, Op.add),
                            ])
                        else:
                            ctx.chain(v, "V", [
                                lambda: v.tensor_copy(EGS[:, :], PB[6][:, 0:128]),
                                lambda: v.tensor_tensor(pe[0:64, :], PB[3][0:64, 0:128],
                                                        EGS[0:64, :], Op.mult),
                                lambda: v.tensor_tensor(pe[64:128, :], PB[3][0:64, 128:256],
                                                        EGS[64:128, :], Op.mult),
                                lambda: v.tensor_tensor(pe[0:64, :], pe[0:64, :],
                                                        PB[2][0:64, 0:128], Op.add),
                                lambda: v.tensor_tensor(pe[64:128, :], pe[64:128, :],
                                                        PB[2][0:64, 128:256], Op.add),
                                lambda: v.scalar_tensor_tensor(
                                    pe, XC[:, cch, si * 128:(si + 1) * 128],
                                    DCOL[:, cch:cch + 1], pe, Op.mult, Op.add),
                            ])
                        ctx.vflush(v, pvd)

                    @block.scalar
                    def _(sc, cch=cch, si=si, ci=ci):
                        sc.wait_ge(pvd.h, pvd.n)
                        dma(sc, pst, p1T[cch * 128:(cch + 1) * 128,
                                         ci * 1024 + si * 128:ci * 1024 + (si + 1) * 128],
                            PEV[:, si % 2, :])

            if DEBUG and ci == 0:
                dbg_xc = ctx.dout("dbg_xc", [2048, 1024], BF16)
                dbg_g8 = ctx.dout("dbg_g8", [128, 1024])
                dbg_agt = ctx.dout("dbg_agt", [128, 256], F32)
                dbg_dtt = ctx.dout("dbg_dtt", [128, 256], F32)
                dbg_wtt = ctx.dout("dbg_wtt", [128, 256], F32)
                dbg_decb = ctx.dout("dbg_decb", [128, 256], F32)
                dbg_bt8 = ctx.dout("dbg_bt8", [128, 1024])

                @block.gpsimd
                def _(g):
                    g.wait_ge(cvs_s.h, cvs_s.n)
                    g.wait_ge(gte.h, gte.n)
                    g.wait_ge(ted.h, ted.n)
                    g.wait_ge(pvd.h, pvd.n)
                    dma(g, ctx.sem("dbgs"), dbg_xc.rearrange("(c p) n -> p c n", p=128),
                        XC[:, :, :])
                    dma(g, ctx.sem("dbgs"), dbg_g8.rearrange("p (s n) -> p s n", s=8), G8[:, :, :])
                    dma(g, ctx.sem("dbgs"), dbg_bt8.rearrange("p (s n) -> p s n", s=8), BT8[:, :, :])
                    dma(g, ctx.sem("dbgs"), dbg_agt.rearrange("p (s n) -> p s n", s=8), AGT[:, :, :])
                    dma(g, ctx.sem("dbgs"), dbg_dtt.rearrange("p (s n) -> p s n", s=8), DTT[:, :, :])
                    dma(g, ctx.sem("dbgs"), dbg_wtt.rearrange("p (s n) -> p s n", s=8), WTT[:, :, :])
                    dma(g, ctx.sem("dbgs"), dbg_decb[:, :], DECB[:, :])
                    dma(g, ctx.sem("dbgs"), ctx.dout("dbg_xcdt", [128, 1024]).rearrange(
                        "p (s n) -> p s n", s=8), XCDT8[:, :, :])
                    dma(g, ctx.sem("dbgs"), ctx.dout("dbg_xcw", [128, 1024]).rearrange(
                        "p (s n) -> p s n", s=8), XCW8[:, :, :])
                    dma(g, ctx.sem("dbgs"), ctx.dout("dbg_mt", [128, 128]), MT[:, :])
                    dma(g, ctx.sem("dbgs"), ctx.dout("dbg_edg", [128, 128]), EDG[:, :])


            @block.gpsimd
            def _(g, ci=ci):
                g.wait_ge(sud.h, sud.n)
                dma(g, cst, schk[ci * 128:(ci + 1) * 128, :], SST[:, :])

            @block.gpsimd
            def _(g, ci=ci):
                g.wait_ge(cst.h, cst.n)
                g.wait_ge(pst.h, pst.n)
                g.wait_ge(zst.h, zst.n)
                fin.inc(g.engine_nop())

        @block.gpsimd
        def _(g):
            g.wait_ge(fin.h, fin.n)

    return ctx


def build_l2b():
    ctx = Ctx("l2b")
    nc = ctx.nc
    sc_in = ctx.din("sc_in", [8 * 128, 2048], F32)
    decrow = ctx.din("decrow", [1, 256], F32)
    c_onesrf = ctx.din("c_onesrf", [1, 128], F32)
    s0out = ctx.dout("s0out", [8 * 128, 2048], F32)

    SC8 = ctx.sb("SC8", [128, 8, 2048], F32)
    S08 = ctx.sb("S08", [128, 8, 2048], F32)
    DECBS = ctx.sb("DECBS", [128, 256], F32)
    DROW = ctx.sb("DROW", [1, 256], F32)
    ONESRF = ctx.sb("ONESRF", [1, 128], F32)
    PBC = ctx.ps("pbc", [128, 512])

    ld = ctx.sem("ld")
    bc = ctx.sem("bc")
    ev = ctx.sem("ev")
    st_ = ctx.sem("st")

    with nc.Block() as block:
        @block.sync
        def _(s):
            dma(s, ld, ONESRF[:, :], c_onesrf[:, :])
            dma(s, ld, DROW[:, :], decrow[:, :])
            for c in range(8):
                dma(s, ld, SC8[:, c, :], sc_in[c * 128:(c + 1) * 128, :])

        @block.tensor
        def _(t):
            t.wait_ge(ld.h, ld.n)
            bc.inc(t.matmul(PBC[:, 0:256], ONESRF, DROW[:, :], start=True, stop=True))

        @block.vector
        def _(v):
            v.wait_ge(bc.h, bc.n)
            v.tensor_copy(DECBS[:, :], PBC[:, 0:256])
            v.memset(S08[:, 0, :], 0.0)
            ctx.vflush(v, ev)
            for c in range(1, 8):
                for h in range(32):
                    v.scalar_tensor_tensor(
                        S08[:, c, h * 64:(h + 1) * 64], S08[:, c - 1, h * 64:(h + 1) * 64],
                        DECBS[:, (c - 1) * 32 + h:(c - 1) * 32 + h + 1],
                        SC8[:, c - 1, h * 64:(h + 1) * 64], Op.mult, Op.add)
                ctx.vflush(v, ev)

        @block.sync
        def _(s):
            s.wait_ge(ev.h, ev.n)
            for c in range(8):
                dma(s, st_, s0out[c * 128:(c + 1) * 128, :], S08[:, c, :])
            s.wait_ge(st_.h, st_.n)

    return ctx


def build_l2c():
    ctx = Ctx("l2c")
    nc = ctx.nc
    xT = ctx.din("xT", [C, NT])
    p1in = ctx.din("p1in", [2048, NT])
    szin = ctx.din("szin", [2048, NT])
    cmin = ctx.din("cmin", [128, NT])
    eagin = ctx.din("eagin", [32, NT], F32)
    s0c = ctx.din("s0c", [256, 2048])
    opT = ctx.din("opT", [2048, C])
    c_hselt = ctx.din("c_hselt", [2, 128], F32)
    c_onesc = ctx.din("c_onesc", [128, 1])
    c_onesr = ctx.din("c_onesr", [1, 128])
    yT = ctx.dout("yT", [C, NT])

    XT = ctx.sb("XT", [128, NCH, NT])
    YF = ctx.sb("YF", [128, 16, 1024])
    S0SB = ctx.sb("S0SB", [128, 2048])
    CMS = ctx.sb("CMS", [128, 1024])
    ERP2 = ctx.sb("ERP2", [2, 1024], F32)
    P1S = ctx.sb("P1S", [128, 2, 512])
    SZS = ctx.sb("SZS", [128, 2, 512])
    HSELT = ctx.sb("HSELT", [2, 128], F32)
    ONESC = ctx.sb("ONESC", [128, 1])
    ONESR = ctx.sb("ONESR", [1, 128])
    SCR = ctx.sb("SCR", [128, 2, 512])
    ROWS = ctx.sb("ROWS", [1, 4, 512])
    OWB = ctx.sb("OWB", [128, 16, 128])
    YCT = ctx.sb("YCT", [128, 512])

    PB = [ctx.ps(f"pb{i}", [128, 512]) for i in range(6)]

    ld = ctx.sem("ld")
    xl = ctx.sem("xl")
    cl = ctx.sem("cl")
    psl = ctx.sem("psl")
    im = ctx.sem("im")
    sel = ctx.sem("sel")
    yfd = ctx.sem("yfd")
    x2d = ctx.sem("x2d")
    fin = ctx.sem("fin")
    st_ = ctx.sem("st")

    with nc.Block() as block:
        @block.sync
        def _(s):
            dma(s, ld, HSELT[:, :], c_hselt[:, :])
            dma(s, ld, ONESC[:, :], c_onesc[:, :])
            dma(s, ld, ONESR[:, :], c_onesr[:, :])
            dma(s, xl, XT[:, :, :], xT.rearrange("(c p) n -> p c n", p=128))

        @block.vector
        def _(v):
            v.wait_ge(xl.h, xl.n)
            for cch in range(NCH):
                v.tensor_scalar(XT[:, cch, :], XT[:, cch, :], 2.0, None, Op.mult)
            ctx.vflush(v, x2d)

        for ci in range(2):
            @block.sync
            def _(s, ci=ci):
                if ci == 1:
                    s.wait_ge(ctx.sem("opev1").h if False else im.h, im.n)
                    s.wait_ge(yfd.h, yfd.n)
                dma(s, cl, S0SB[:, :], s0c[ci * 128:(ci + 1) * 128, :])
                dma(s, cl, CMS[:, :], cmin[:, ci * 1024:(ci + 1) * 1024])

            for cch in range(16):
                @block.sync
                def _(s, cch=cch, ci=ci):
                    s.wait_ge(sel.h, sel.n)
                    dma(s, cl, ERP2[:, :], eagin[2 * cch:2 * cch + 2,
                                                 ci * 1024:(ci + 1) * 1024])

                for q in range(2):
                    @block.sync
                    def _(s, cch=cch, q=q, ci=ci):
                        s.wait_ge(yfd.h, max(0, yfd.n - 1))
                        dma(s, psl, P1S[:, q % 2, :],
                            p1in[cch * 128:(cch + 1) * 128,
                                 ci * 1024 + q * 512:ci * 1024 + (q + 1) * 512])
                        dma(s, psl, SZS[:, q % 2, :],
                            szin[cch * 128:(cch + 1) * 128,
                                 ci * 1024 + q * 512:ci * 1024 + (q + 1) * 512])

                    @block.tensor
                    def _(t, cch=cch, q=q):
                        t.wait_ge(cl.h, cl.n)
                        t.wait_ge(yfd.h, yfd.n)
                        for hh in range(2):
                            h = 2 * cch + hh
                            im.inc(t.matmul((PB[3] if hh == 0 else PB[4])[0:64, :],
                                            S0SB[:, h * 64:(h + 1) * 64],
                                            CMS[:, q * 512:(q + 1) * 512],
                                            start=True, stop=True))
                        sel.inc(t.matmul(PB[2][:, :], HSELT,
                                         ERP2[0:2, q * 512:(q + 1) * 512],
                                         start=True, stop=True))

                    @block.vector
                    def _(v, cch=cch, q=q, ci=ci, gate=(ctx.sem("op0mm").n if ci == 1 else None)):
                        if gate is not None and cch == 0 and q == 0:
                            v.wait_ge(ctx.sem("op0mm").h, gate)
                        v.wait_ge(im.h, im.n)
                        v.wait_ge(sel.h, sel.n)
                        v.wait_ge(psl.h, psl.n)
                        dst = YF[:, cch, q * 512:(q + 1) * 512]
                        ctx.chain(v, "V", [
                            lambda: v.tensor_copy(YCT[0:64, :], PB[3][0:64, :]),
                            lambda: v.tensor_copy(YCT[64:128, :], PB[4][0:64, :]),
                            lambda: v.tensor_tensor(dst, YCT[:, :], PB[2][:, :], Op.mult),
                            lambda: v.tensor_tensor(dst, dst, P1S[:, q % 2, :], Op.add),
                            lambda: v.tensor_tensor(dst, dst, SZS[:, q % 2, :], Op.mult),
                        ])
                        ctx.vflush(v, yfd)

            ev = emit_projF(ctx, block, f"op{ci}", YF, opT,
                            XT[:, :, ci * 1024:(ci + 1) * 1024], 2048, C, ntok=1024,
                            wait=[(yfd, yfd.n), (x2d, x2d.n)], add_into=True, wbuf=OWB,
                            psums=[PB[0], PB[1]])

        @block.vector
        def _(v):
            v.wait_ge(ctx.sem("op0ev").h, ctx.sem("op0ev").n)
            v.wait_ge(ctx.sem("op1ev").h, ctx.sem("op1ev").n)
            ctx.vflush(v, ctx.sem("lnpre"))

        emit_ln(ctx, block, XT, SCR, ROWS, ONESC, ONESR,
                [PB[0], PB[1], PB[4], PB[5]],
                ctx.sem("lnpre"), ctx.sem("lnpre").n, ctx.sem("lnd"), ntok=NT)

        @block.sync
        def _(s):
            s.wait_ge(ctx.sem("lnd").h, ctx.sem("lnd").n)
            dma(s, st_, yT.rearrange("(c p) n -> p c n", p=128), XT[:, :, :])
            s.wait_ge(st_.h, st_.n)

    return ctx


def launch_l2attn(x, norm_w, in_proj_w, conv_w, conv_b, A_log, D, dt_bias, out_proj_w):
    NTK = L2_NTK
    HALO = L2_HALO
    inwT = np.ascontiguousarray(in_proj_w.T.astype(np.float32))
    opT = np.ascontiguousarray(out_proj_w.T.astype(np.float32))
    nw_c = np.ascontiguousarray(norm_w.reshape(NCH, 128).T.astype(np.float32))
    wcv_c = np.ascontiguousarray(
        conv_w[:, 0, :].reshape(16, 128, 4).transpose(1, 0, 2).reshape(128, 64).astype(np.float32))
    cvb_c = np.ascontiguousarray(conv_b.reshape(16, 128).T.astype(np.float32))
    dcol_c = np.ascontiguousarray(np.repeat(D, 64).reshape(16, 128).T.astype(np.float32))
    alog_c = np.ascontiguousarray(A_log.reshape(32, 1).astype(np.float32))
    dtb_c = np.ascontiguousarray(dt_bias.reshape(32, 1).astype(np.float32))
    ident = np.eye(128, dtype=np.float32)
    hselt = np.zeros((2, 128), np.float32)
    hselt[0, 0:64] = 1.0
    hselt[1, 64:128] = 1.0

    ctxa = build_l2a()
    maps = []
    for c in range(NCORES):
        bb, sl = core_slices(c, True)
        xh = np.zeros((C, 2 * NTK), np.float32)
        for ci, (s0, s1) in enumerate(sl):
            lo = s0 - HALO
            pad = max(0, -lo)
            xh[:, ci * NTK + pad:(ci + 1) * NTK] = x[bb, lo + pad:s1, :].T
        maps.append({
            "xh": xh, "inwT": inwT, "nw_c": nw_c, "wcv_c": wcv_c, "cvb_c": cvb_c,
            "dcol_c": dcol_c, "alog_c": alog_c, "dtb_c": dtb_c,
            "ones32": np.ones((32, 1024), np.float32),
            "c_one32": np.ones((32, 1), np.float32),
            "c_onesc": np.ones((128, 1), np.float32),
            "c_onesr": np.ones((1, 128), np.float32),
            "c_onesrf": np.ones((1, 128), np.float32),
            "c_hselt": hselt, "c_ident": ident, "c_identr": ident})
    resa = run_bass_kernel_spmd(ctxa.nc, maps, core_ids=list(range(NCORES)))

    def gmap(bb, g):
        j = g if g < 4 else 7 - g
        ci = 0 if g < 4 else 1
        return bb * 4 + j, ci

    ctxb = build_l2b()
    mapsb = []
    for bb in range(B):
        sc_in = np.zeros((8 * 128, 2048), np.float32)
        decrow = np.zeros((1, 256), np.float32)
        for g in range(8):
            core, ci = gmap(bb, g)
            sc_in[g * 128:(g + 1) * 128, :] = resa.results[core]["schk"][ci * 128:(ci + 1) * 128, :]
            decrow[0, g * 32:(g + 1) * 32] = resa.results[core]["eagr"][:, ci * 1024 + 1023]
        mapsb.append({"sc_in": sc_in, "decrow": decrow,
                      "c_onesrf": np.ones((1, 128), np.float32)})
    resb = run_bass_kernel_spmd(ctxb.nc, mapsb, core_ids=[0, 1])

    ctxc = build_l2c()
    mapsc = []
    for c in range(NCORES):
        bb, sl = core_slices(c, True)
        s0c = np.zeros((256, 2048), np.float32)
        for ci, (s0, s1) in enumerate(sl):
            g = s0 // 1024
            s0c[ci * 128:(ci + 1) * 128, :] = resb.results[bb]["s0out"][g * 128:(g + 1) * 128, :]
        xTc = np.concatenate([x[bb, s0:s1, :].T for (s0, s1) in sl], axis=1)
        mapsc.append({
            "xT": np.ascontiguousarray(xTc),
            "p1in": resa.results[c]["p1T"], "szin": resa.results[c]["szT"],
            "cmin": resa.results[c]["cmT"], "eagin": resa.results[c]["eagr"],
            "s0c": s0c, "opT": opT, "c_hselt": hselt,
            "c_onesc": np.ones((128, 1), np.float32),
            "c_onesr": np.ones((1, 128), np.float32)})
    resc = run_bass_kernel_spmd(ctxc.nc, mapsc, core_ids=list(range(NCORES)))

    out = np.empty_like(x)
    for c in range(NCORES):
        bb, sl = core_slices(c, True)
        o = 0
        for (s0, s1) in sl:
            out[bb, s0:s1, :] = resc.results[c]["yT"][:, o:o + (s1 - s0)].T
            o += s1 - s0
    return out


def kernel(**inputs):
    g = lambda n: np.ascontiguousarray(np.asarray(inputs[n]))
    x = g('L0_attn_x')

    W0a = {k: g('L0_attn_' + k) for k in
           ['wq_d', 'bq_d', 'w_qk', 'b_qk', 'wkv_d', 'bkv_d', 'wv_u', 'bv_u',
            'w_proj', 'b_proj']}
    x = launch_l0attn(x, W0a)

    W0f = {k: g('L0_ffn_' + k) for k in
           ['wqkv', 'q_scale', 'k_scale', 'wo', 'ln_g', 'ln_b', 'w1', 'b1', 'w2', 'b2']}
    x = launch_lffn(x, W0f, 'L0')

    qT, kT, vv, qn, kn = launch_l1attn_a(
        x, g('L1_attn_Wq'), g('L1_attn_bq'), g('L1_attn_Wk'), g('L1_attn_bk'),
        g('L1_attn_Wv'), g('L1_attn_bv'))
    x = launch_l1attn_b(x, qT, kT, vv, qn, kn, g('L1_attn_Wo'), g('L1_attn_bo'))

    x = launch_l1ffn(x, g('L1_ffn_w_gate'), g('L1_ffn_w_up'), g('L1_ffn_w_down'))

    x = launch_l2attn(x, g('L2_attn_norm_w'), g('L2_attn_in_proj_w'),
                      g('L2_attn_conv_w'), g('L2_attn_conv_b'), g('L2_attn_A_log'),
                      g('L2_attn_D'), g('L2_attn_dt_bias'), g('L2_attn_out_proj_w'))

    x = launch_l2ffn(x, g('L2_ffn_w1'), g('L2_ffn_w2'), g('L2_ffn_router_w'))

    x = launch_l3attn(x, g('L3_attn_W_attn'), g('L3_attn_v_tmp'), g('L3_attn_proj_tmp'))

    W3f = {k: g('L3_ffn_' + k) for k in
           ['wqkv', 'q_scale', 'k_scale', 'wo', 'ln_g', 'ln_b', 'w1', 'b1', 'w2', 'b2']}
    x = launch_lffn(x, W3f, 'L3')
    return np.ascontiguousarray(x.astype(np.float32))
